# version 03
# baseline (speedup 1.0000x reference)
import math
from contextlib import ExitStack

import numpy as np

import concourse.bass as bass
import concourse.mybir as mybir
from concourse.bass_utils import run_bass_kernel_spmd

F32 = mybir.dt.float32
BF16 = mybir.dt.bfloat16
AF = mybir.ActivationFunctionType
ALU = mybir.AluOpType
AX = mybir.AxisListType

import os as _os
DEV_DUMP = False
A1_TILES = int(_os.environ.get("A1_TILES", "8"))
A1_QS = int(_os.environ.get("A1_QS", "32"))
A1_LVL = int(_os.environ.get("A1_LVL", "5"))
LF_SUBS = int(_os.environ.get("LF_SUBS", "12"))
LF_SKIP = _os.environ.get("LF_SKIP", "")
STRICT_SYNC = False
STRICT_STREAM = True
STRICT_PEPE = False
STRICT_WARWAW = False

D = 1024
S = 4096
NCH = 8
TT = 512
NTT = S // TT
NCORES = 8
LN_EPS = 1e-5

ENGINES = ("pe", "act", "dve", "pool", "sp")
SEM_CAP = 48000
DMA_POOL = 10
DMA_CAP = 2900



class _Op:
    __slots__ = ("fn", "kind", "tick", "deps", "dbg", "ordn")

    def __init__(self, fn, kind, dbg):
        self.fn = fn
        self.kind = kind
        self.tick = False
        self.deps = {}
        self.dbg = dbg
        self.ordn = -1


def _space(kind):
    return "d" if kind == "dma" else "c"


class Prog:
    def __init__(self):
        self.ops = {e: [] for e in ENGINES}
        self.writer = {}
        self.readers = {}
        self.dma_count = {e: 0 for e in ENGINES}
        self.strict = STRICT_SYNC

    def emit(self, eng, fn, reads=(), writes=(), kind="stream", sreads=(), dbg=""):
        assert eng in ENGINES
        idx = len(self.ops[eng])
        op = _Op(fn, kind, dbg)

        def dep(weng, widx, wkind, force=False):
            need = True
            if not force and not self.strict and weng == eng:
                if wkind == "dma" or kind == "dma":
                    need = True
                elif eng == "pe":
                    need = STRICT_PEPE
                elif wkind == "end":
                    need = True
                else:
                    need = STRICT_STREAM
            if need:
                wop = self.ops[weng][widx]
                wop.tick = True
                if wkind == "dma":
                    key = (weng, "d", wop.ordn % DMA_POOL)
                else:
                    key = (weng, "c")
                if op.deps.get(key, -1) < widx:
                    op.deps[key] = widx

        for r in reads:
            w = self.writer.get(r)
            if w:
                dep(*w)
            if r.startswith("ps_"):
                for (reng, ridx, rkind) in self.readers.get(r, ()):
                    if reng != eng:
                        dep(reng, ridx, rkind)
        for r in sreads:
            w = self.writer.get(r)
            if w:
                dep(*w, force=(w[0] == eng and w[2] != "dma"))
        for wr in writes:
            w = self.writer.get(wr)
            if w:
                weng, widx, wkind = w
                if (weng != eng or wkind == "dma" or kind == "dma" or self.strict
                        or STRICT_WARWAW):
                    dep(weng, widx, wkind)
            for (reng, ridx, rkind) in self.readers.get(wr, ()):
                if (reng != eng or rkind == "dma" or kind == "dma" or self.strict
                        or STRICT_WARWAW):
                    dep(reng, ridx, rkind)

        for r in tuple(reads) + tuple(sreads):
            self.readers.setdefault(r, []).append((eng, idx, kind))
        for wr in writes:
            self.writer[wr] = (eng, idx, kind)
            self.readers[wr] = []

        if kind == "dma":
            op.ordn = self.dma_count[eng]
            self.dma_count[eng] += 1
        self.ops[eng].append(op)
        return (eng, idx)

    def barrier(self):
        last = {}
        for e in ENGINES:
            for i in range(len(self.ops[e]) - 1, -1, -1):
                if _space(self.ops[e][i].kind) == "c":
                    last[(e, "c")] = i
                    break
            found = 0
            for i in range(len(self.ops[e]) - 1, -1, -1):
                if self.ops[e][i].kind == "dma":
                    key = (e, "d", self.ops[e][i].ordn % DMA_POOL)
                    if key not in last:
                        last[key] = i
                        found += 1
                        if found == DMA_POOL:
                            break
        for e in ENGINES:
            op = _Op(lambda en: en.nop(), "stream", "barrier")
            for key, li in last.items():
                if key == (e, "c"):
                    continue
                self.ops[key[0]][li].tick = True
                if op.deps.get(key, -1) < li:
                    op.deps[key] = li
            self.ops[e].append(op)

    def finalize(self, nc, es):
        tickno = {}
        counts = {}
        dma_ord = {(e, i): op.ordn for e in ENGINES
                   for i, op in enumerate(self.ops[e]) if op.kind == "dma"}
        dma_n = {e: self.dma_count[e] for e in ENGINES if self.dma_count[e]}
        for e in ENGINES:
            for i, op in enumerate(self.ops[e]):
                if op.kind == "dma":
                    op.tick = True
                elif op.tick:
                    c = counts.get(e, 0) + 1
                    counts[e] = c
                    tickno[(e, i)] = c

        csems = {}
        for e, total in counts.items():
            n = (total + SEM_CAP - 1) // SEM_CAP
            csems[e] = [es.enter_context(nc.semaphore(f"s_{e}_c_{k}"))
                        for k in range(n)]
        dsems = {}
        for e, total in dma_n.items():
            k = min(DMA_POOL, total)
            dsems[e] = [es.enter_context(nc.semaphore(f"s_{e}_d_{k2}"))
                        for k2 in range(k)]
            assert (total + DMA_POOL - 1) // DMA_POOL <= DMA_CAP, "dma sem overflow"

        def cval(e, tick):
            epoch = (tick - 1) // SEM_CAP
            return csems[e][epoch], tick - epoch * SEM_CAP

        def dval(e, ordn):
            return dsems[e][ordn % DMA_POOL], 16 * (ordn // DMA_POOL + 1)

        def make_runner(eng):
            def run(engobj):
                hw = {}
                for i, op in enumerate(self.ops[eng]):
                    for (key, widx) in sorted(op.deps.items(), key=lambda kv: str(kv[0])):
                        if len(key) == 3:
                            sem, val = dval(key[0], dma_ord[(key[0], widx)])
                        else:
                            sem, val = cval(key[0], tickno[(key[0], widx)])
                        k2 = id(sem)
                        if hw.get(k2, -1) < val:
                            engobj.wait_ge(sem, val)
                            hw[k2] = val
                    if op.kind == "dma":
                        ordn = dma_ord[(eng, i)]
                        sem, val = dval(eng, ordn)
                        if ordn >= DMA_POOL:
                            if hw.get(id(sem), -1) < val - 16:
                                engobj.wait_ge(sem, val - 16)
                                hw[id(sem)] = val - 16
                        inst = op.fn(engobj)
                        inst.then_inc(sem, 16)
                    else:
                        inst = op.fn(engobj)
                        if op.tick:
                            sem, _ = cval(eng, tickno[(eng, i)])
                            inst.then_inc(sem, 1)
            return run

        return {e: make_runner(e) for e in ENGINES}


ARENA_BYTES = 184 * 1024


class Cx:
    def __init__(self, nc):
        self.nc = nc
        self.P = Prog()
        self.t = {}
        self.arena_base = None
        self.ph_off = 0
        self.sub_base = 0
        self.sub_off = 0
        self.uid = 0

    def init_arena(self):
        h = self.nc.alloc_sbuf_tensor("arena", [128, ARENA_BYTES], mybir.dt.int8)
        self.arena_base = self.nc.lookup_mloc(h).addr

    def phase_reset(self):
        self.ph_off = 0
        self.sub_base = 0
        self.sub_off = 0

    def _alloc_at(self, name, shape, dtype, off):
        assert name not in self.t, f"tensor name collision: {name}"
        nbytes = int(np.prod(shape[1:])) * mybir.dt.size(dtype)
        nbytes = (nbytes + 31) // 32 * 32
        self.uid += 1
        h = self.nc.alloc_sbuf_tensor_at(f"{name}_{self.uid}", shape, dtype,
                                         offset=self.arena_base + off)
        self.t[name] = h
        return h, off + nbytes

    def sbp(self, name, shape, dtype=F32):
        h, self.ph_off = self._alloc_at(name, shape, dtype, self.ph_off)
        if self.ph_off > ARENA_BYTES:
            raise MemoryError(f"arena phase overflow at {name}: {self.ph_off}")
        self.sub_base = max(self.sub_base, self.ph_off)
        return h

    def sub_reset(self):
        self.sub_base = max(self.sub_base, self.ph_off)
        self.sub_off = self.sub_base

    def sbs(self, name, shape, dtype=F32):
        if self.sub_off < self.sub_base:
            self.sub_off = self.sub_base
        h, self.sub_off = self._alloc_at(name, shape, dtype, self.sub_off)
        if self.sub_off > ARENA_BYTES:
            raise MemoryError(f"arena sub overflow at {name}: {self.sub_off}")
        return h

    def sub_mark(self):
        return self.sub_off

    def sub_release(self, mark):
        self.sub_off = mark

    def sb(self, name, shape, dtype=F32):
        assert name not in self.t, f"tensor name collision: {name}"
        h = self.nc.alloc_sbuf_tensor(name, shape, dtype)
        self.t[name] = h
        return h

    def dram(self, name, shape, dtype=F32, kind=None):
        assert name not in self.t, f"tensor name collision: {name}"
        if kind:
            h = self.nc.dram_tensor(name, shape, dtype, kind=kind)
        else:
            h = self.nc.dram_tensor(name, shape, dtype)
        self.t[name] = h
        return h


def ts(c, w):
    return slice(c * w, (c + 1) * w)


def cm3(sb_handle, total_w, chunk_w, nch=8, off=0):
    return bass.AP(sb_handle, off, [[total_w, 128], [total_w // nch, nch], [1, chunk_w]])


def fm3(dram_handle, t0, w, p=128):
    return dram_handle.ap()[:, t0:t0 + w].rearrange("(c p) t -> p c t", p=p)



def _pt(vec):
    v = np.asarray(vec, dtype=np.float32).reshape(-1)
    n = v.size // 128
    return np.ascontiguousarray(v.reshape(n, 128).T)


def host_prep(inputs):
    h = {}
    f32 = lambda a: np.ascontiguousarray(np.asarray(a, dtype=np.float32))

    w_qkv = f32(inputs["L0_attn_w_qkv"])
    h["l0_wqkvT"] = f32(w_qkv.T)
    h["l0_woutT"] = f32(np.asarray(inputs["L0_attn_w_out"]).T)
    h["l0_bout_pt"] = _pt(inputs["L0_attn_b_out"])
    h["l0_bconv_pt"] = _pt(inputs["L0_attn_b_conv"])
    h["l0_nullk_pt"] = _pt(np.asarray(inputs["L0_attn_null_k"]).reshape(-1))
    h["l0_nullv_pt"] = _pt(np.asarray(inputs["L0_attn_null_v"]).reshape(-1))
    w_conv = f32(inputs["L0_attn_w_conv"])
    convw = np.zeros((3, 8, 128, 128), dtype=np.float32)
    for tau in range(3):
        for c in range(8):
            for half in range(2):
                oc = 128 * c + 64 * half
                convw[tau, c, 64 * half:64 * half + 64, 64 * half:64 * half + 64] = \
                    w_conv[oc:oc + 64, :, tau].T
    h["l0_convw"] = np.ascontiguousarray(convw.transpose(2, 0, 1, 3).reshape(128, 3 * 8 * 128))

    J = S // 3 + 1 + 1
    prefix, ncol = [], []
    band = np.zeros((32, 128, 64), dtype=np.float32)
    for Q in range(32):
        i0 = 128 * Q
        p = i0 // 3 + 2
        n_ = (i0 + 127) // 3 + 2
        prefix.append(p)
        ncol.append(n_)
        for r in range(128):
            i = i0 + r
            for c in range(n_ - p):
                j = (p + c) - 1
                if i < 3 * j:
                    band[Q, r, c] = -1e30
    h["l0_band"] = np.ascontiguousarray(band.transpose(1, 0, 2).reshape(128, 32 * 64))
    h["_l0_prefix"] = prefix
    h["_l0_ncol"] = ncol
    h["_l0_J"] = J

    maker = f32(inputs["L0_ffn_rail_maker"])
    up = f32(inputs["L0_ffn_rail_up_proj"])
    rail1 = np.zeros((128, 4 * 8 * 128), dtype=np.float32)
    for hd in range(4):
        for yc in range(8):
            for rr in range(8):
                r = 8 * yc + rr
                rail1[rr * 16:rr * 16 + 16,
                      (hd * 8 + yc) * 128 + rr * 16:(hd * 8 + yc) * 128 + rr * 16 + 16] =                     maker[hd, r]
    h["l0_rail1"] = rail1
    rail2 = np.zeros((128, 4 * 32 * 128), dtype=np.float32)
    for hd in range(4):
        for zc in range(32):
            base_r = (zc % 4) * 32
            col0 = (hd * 32 + zc) * 128
            for par in range(2):
                rail2[base_r + par * 16: base_r + par * 16 + 16,
                      col0 + par * 64: col0 + par * 64 + 64] = up[hd, 2 * zc + par].T
    h["l0_rail2"] = rail2
    h["l0_wcpT"] = f32(np.asarray(inputs["L0_ffn_w_cproj"]).T)

    h["l1_w1T"] = f32(np.asarray(inputs["L1_attn_W1"]).T)
    h["l1_w2T"] = f32(np.asarray(inputs["L1_attn_W2"]).T)
    h["l1_w3T"] = f32(np.asarray(inputs["L1_attn_W3"]).T)
    h["l1_woT"] = f32(np.asarray(inputs["L1_attn_Wo"]).T)
    invf = 1.0 / 10000.0 ** (np.arange(0, 128, 2, dtype=np.float64) / 128.0)
    invf2 = np.repeat(invf, 2)
    cosv = np.zeros((128, S), dtype=np.float32)
    sinv = np.zeros((128, S), dtype=np.float32)
    for fc in range(8):
        tau = np.arange(512, dtype=np.float64)
        sp = 8.0 * tau + fc
        ang = sp[None, :] * invf2[:, None]
        cosv[:, fc * 512:(fc + 1) * 512] = np.cos(ang)
        sinv[:, fc * 512:(fc + 1) * 512] = np.sin(ang)
    h["l1_cos"] = cosv
    h["l1_sin"] = sinv
    ropeP = np.zeros((128, 128), dtype=np.float32)
    for i in range(64):
        ropeP[2 * i + 1, 2 * i] = -1.0
        ropeP[2 * i, 2 * i + 1] = 1.0
    h["ropeP"] = ropeP

    DIN = 2730
    DINP = 2816
    for i in range(6):
        h[f"lf_wqkv{i}"] = f32(inputs["L1_ffn_Wqkv"][i])
        h[f"lf_wo{i}"] = f32(inputs["L1_ffn_Wo"][i])
        w1 = f32(inputs["L1_ffn_W1"][i])
        w1p = np.zeros((1024, 2 * DINP), dtype=np.float32)
        w1p[:, :DIN] = w1[:, :DIN]
        w1p[:, DINP:DINP + DIN] = w1[:, DIN:]
        h[f"lf_w1p{i}"] = w1p
        w2 = f32(inputs["L1_ffn_W2"][i])
        w2p = np.zeros((DINP, 1024), dtype=np.float32)
        w2p[:DIN] = w2
        h[f"lf_w2p{i}"] = w2p
    b1 = f32(inputs["L1_ffn_b1"])
    b1p = np.zeros((6, 2 * DINP), dtype=np.float32)
    b1p[:, :DIN] = b1[:, :DIN]
    b1p[:, DINP:DINP + DIN] = b1[:, DIN:]
    h["lf_b1pt"] = np.ascontiguousarray(
        b1p.reshape(6, 44, 128).transpose(2, 0, 1).reshape(128, 6 * 44))
    h["lf_b2pt"] = np.ascontiguousarray(
        f32(inputs["L1_ffn_b2"]).reshape(6, 8, 128).transpose(2, 0, 1).reshape(128, 48))
    h["lf_gpt"] = np.ascontiguousarray(
        f32(inputs["L1_ffn_ln_g"]).reshape(6, 8, 128).transpose(2, 0, 1).reshape(128, 48))
    h["lf_bpt"] = np.ascontiguousarray(
        f32(inputs["L1_ffn_ln_b"]).reshape(6, 8, 128).transpose(2, 0, 1).reshape(128, 48))
    qs_ = f32(inputs["L1_ffn_q_scale"])
    ks_ = f32(inputs["L1_ffn_k_scale"])
    h["lf_qspt"] = np.ascontiguousarray(np.tile(qs_.T, (2, 1)))
    h["lf_kspt"] = np.ascontiguousarray(np.tile(ks_.T, (2, 1)))
    DH = 64
    half = np.arange(0, DH, 2, dtype=np.float64)
    invf_l = 1.0 / 10000.0 ** (half / DH)
    base = (half + 0.4 * DH) / (1.4 * DH)
    tpos = np.arange(S, dtype=np.float64)
    fr64 = tpos[:, None] * invf_l[None, :]
    fr64 = np.concatenate([fr64, fr64], axis=1)
    power = (tpos - S // 2) / 128.0
    sc64 = base[None, :] ** power[:, None]
    sc64 = np.concatenate([sc64, sc64], axis=1)
    cos64 = np.cos(fr64)
    sin64 = np.sin(fr64)
    mk = lambda a: np.ascontiguousarray(
        np.tile(a.T.astype(np.float32), (2, 1)))
    h["xp_cosS"] = mk(cos64 * sc64)
    h["xp_sinS"] = mk(sin64 * sc64)
    h["xp_cosI"] = mk(cos64 / sc64)
    h["xp_sinI"] = mk(sin64 / sc64)
    ropeH = np.zeros((128, 128), dtype=np.float32)
    for blk in range(2):
        for d in range(32):
            ropeH[blk * 64 + d + 32, blk * 64 + d] = -1.0
            ropeH[blk * 64 + d, blk * 64 + d + 32] = 1.0
    h["lf_ropeH"] = ropeH
    o2 = np.zeros((128, 2), dtype=np.float32)
    o2[:64, 0] = 1.0
    o2[64:, 1] = 1.0
    h["lf_ones2"] = o2
    h["lf_ones2T"] = np.ascontiguousarray(o2.T)
    msk = np.zeros((128, 3 * 384), dtype=np.float32)
    for r in range(128):
        for c in range(384):
            valid = (r <= c <= r + 256)
            msk[r, c] = 0.0 if valid else -1e30
            msk[r, 384 + c] = 0.0 if (valid and c >= 256) else -1e30
            msk[r, 768 + c] = 0.0 if (valid and c >= 128) else -1e30
    h["lf_mask"] = msk

    h["ident_bf"] = np.eye(128, dtype=np.float32)
    h["ones_col"] = np.ones((128, 8), dtype=np.float32)
    h["ones_row"] = np.ones((1, 128), dtype=np.float32)
    cv = np.zeros((128, 8), dtype=np.float32)
    cv[:, 0] = LN_EPS
    cv[:, 1] = -0.5
    cv[:, 2] = 1.0 / 32.0
    cv[:, 3] = 1.0 / math.sqrt(8.0)
    cv[:, 4] = 1e-24
    cv[:, 5] = 8.0
    h["cvals"] = cv
    return h



def _ln_stats(cx, name, src_res, src_aps, sq_res, sq_aps, n):
    P, t = cx.P, cx.t
    PS = t["psum"]
    ones = t["c_ones_bf"]
    s1 = PS["S1"][0:1, 0:n]
    s2 = PS["S2"][0:1, 0:n]
    for c in range(8):
        P.emit("pe", (lambda a, c=c: (lambda e: e.matmul(
            s1, ones[:, 0:1], a, start=(c == 0), stop=(c == 7))))(src_aps[c]),
            reads=[f"{src_res}{c}", "c_ones"],
            writes=["ps_S1"] if c in (0, 7) else [],
            kind="mm", dbg=f"{name}_s1_{c}")
    for c in range(8):
        P.emit("pe", (lambda a, c=c: (lambda e: e.matmul(
            s2, ones[:, 0:1], a, start=(c == 0), stop=(c == 7))))(sq_aps[c]),
            reads=[f"{sq_res}{c}", "c_ones"],
            writes=["ps_S2"] if c in (0, 7) else [],
            kind="mm", dbg=f"{name}_s2_{c}")
    return _ln_finish(cx, name, n)


def _ln_finish(cx, name, n):
    P, t = cx.P, cx.t
    PS = t["psum"]
    s1 = PS["S1"][0:1, 0:n]
    s2 = PS["S2"][0:1, 0:n]
    mu, ms, var, rr, nmr, tmp = (t["st_mu"], t["st_ms"], t["st_var"],
                                 t["st_r"], t["st_nmr"], t["st_tmp"])
    P.emit("dve", lambda e: e.tensor_scalar(mu[:, :n], s1, 1.0 / D, None, op0=ALU.mult),
           reads=["ps_S1"], writes=["st_mu"], dbg=f"{name}_mu")
    P.emit("dve", lambda e: e.tensor_scalar(ms[:, :n], s2, 1.0 / D, None, op0=ALU.mult),
           reads=["ps_S2"], writes=["st_ms"], dbg=f"{name}_ms")
    P.emit("dve", lambda e: e.tensor_tensor(tmp[:, :n], mu[:, :n], mu[:, :n], op=ALU.mult),
           reads=["st_mu"], writes=["st_tmp"], dbg=f"{name}_mu2")
    P.emit("dve", lambda e: e.tensor_tensor(var[:, :n], ms[:, :n], tmp[:, :n], op=ALU.subtract),
           reads=["st_ms", "st_tmp"], writes=["st_var"], dbg=f"{name}_var")
    cvals = t["c_vals"]
    P.emit("act", lambda e: e.activation(tmp[:, :n], var[:, :n], AF.Ln,
                                         bias=cvals[0:1, 0:1], scale=1.0),
           reads=["st_var", "c_vals"], writes=["st_tmp"], kind="end", dbg=f"{name}_ln")
    P.emit("act", lambda e: e.activation(rr[:, :n], tmp[:, :n], AF.Exp,
                                         bias=0.0, scale=cvals[0:1, 1:2]),
           reads=["st_tmp", "c_vals"], writes=["st_r"], kind="end", dbg=f"{name}_exp")
    P.emit("dve", lambda e: e.scalar_tensor_tensor(
        nmr[:, :n], mu[:, :n], -1.0, rr[:, :n], op0=ALU.mult, op1=ALU.mult),
        reads=["st_mu", "st_r"], writes=["st_nmr"], dbg=f"{name}_nmr")
    onesr = t["c_ones_row"]
    B1 = PS["S1"][:, 0:n]
    B2 = PS["S2"][:, 0:n]
    P.emit("pe", lambda e: e.matmul(B1, onesr[:, :], rr[:1, :n], start=True, stop=True),
           reads=["st_r", "c_ones_row"], writes=["ps_S1"], kind="mm", dbg=f"{name}_b1")
    P.emit("pe", lambda e: e.matmul(B2, onesr[:, :], nmr[:1, :n], start=True, stop=True),
           reads=["st_nmr", "c_ones_row"], writes=["ps_S2"], kind="mm", dbg=f"{name}_b2")
    return B1, B2


def _preln_tile(cx, pfx, tt, x_in, xt, xbf, xsq, xnbf, xtmp, apply=True):
    P, t = cx.P, cx.t
    xsl = fm3(x_in, tt * TT, TT)
    P.emit("sp", (lambda xsl=xsl: (lambda e: e.dma_start(
        cm3(xt, NCH * TT, TT), xsl)))(),
        writes=[f"{pfx}_xt{c}" for c in range(8)], kind="dma", dbg=f"{pfx}ldx{tt}")
    for c in range(8):
        P.emit("dve", (lambda c=c: (lambda e: e.tensor_copy(
            xbf[:, ts(c, TT)], xt[:, ts(c, TT)])))(),
            reads=[f"{pfx}_xt{c}"], writes=[f"{pfx}_xbf{c}"], dbg=f"{pfx}cx{tt}_{c}")
        P.emit("act", (lambda c=c: (lambda e: e.activation(
            xsq[:, ts(c, TT)], xt[:, ts(c, TT)], AF.Square, bias=0.0, scale=1.0)))(),
            reads=[f"{pfx}_xt{c}"], writes=[f"{pfx}A_sq{c}"], dbg=f"{pfx}sx{tt}_{c}")
    B1, B2 = _ln_stats(cx, f"{pfx}A{tt}", f"{pfx}_xbf",
                       [xbf[:, ts(c, TT)] for c in range(8)],
                       f"{pfx}A_sq", [xsq[:, ts(c, TT)] for c in range(8)], TT)
    if not apply:
        return
    for c in range(8):
        P.emit("dve", (lambda c=c, B1=B1: (lambda e: e.scalar_tensor_tensor(
            xtmp[:, :], xt[:, ts(c, TT)], 1.0, B1, op0=ALU.mult, op1=ALU.mult)))(),
            reads=[f"{pfx}_xt{c}", "ps_S1"], writes=[f"{pfx}_xtmp"], dbg=f"{pfx}lm{tt}_{c}")
        P.emit("dve", (lambda c=c, B2=B2: (lambda e: e.tensor_tensor(
            xnbf[:, ts(c, TT)], xtmp[:, :], B2, op=ALU.add)))(),
            reads=[f"{pfx}_xtmp", "ps_S2"], writes=[f"{pfx}_xn{c}"], dbg=f"{pfx}la{tt}_{c}")


def _sandwich_epilogue(cx, pfx, tt, osb, osq, obf, xt, xtmp, x_in, x_out,
                       osb_res=None):
    P, t = cx.P, cx.t
    osb_res = osb_res or [f"{pfx}_osb{c}" for c in range(8)]
    for c in range(8):
        P.emit("act", (lambda c=c: (lambda e: e.activation(
            osq[:, ts(c, TT)], osb[:, ts(c, TT)], AF.Square, bias=0.0, scale=1.0)))(),
            reads=[osb_res[c]], writes=[f"{pfx}E_sq{c}"], dbg=f"{pfx}os{tt}_{c}")
        P.emit("pool", (lambda c=c: (lambda e: e.tensor_copy(
            obf[:, ts(c, TT)], osb[:, ts(c, TT)])))(),
            reads=[osb_res[c]], writes=[f"{pfx}_obf{c}"], kind="end",
            dbg=f"{pfx}oc{tt}_{c}")
    B1, B2 = _ln_stats(cx, f"{pfx}E{tt}", f"{pfx}_obf",
                       [obf[:, ts(c, TT)] for c in range(8)],
                       f"{pfx}E_sq", [osq[:, ts(c, TT)] for c in range(8)], TT)
    xsl = fm3(x_in, tt * TT, TT)
    P.emit("sp", (lambda xsl=xsl: (lambda e: e.dma_start(
        cm3(xt, NCH * TT, TT), xsl)))(),
        writes=[f"{pfx}_xt{c}" for c in range(8)], kind="dma", dbg=f"{pfx}l2x{tt}")
    for c in range(8):
        P.emit("dve", (lambda c=c, B1=B1: (lambda e: e.scalar_tensor_tensor(
            xtmp[:, :], osb[:, ts(c, TT)], 1.0, B1, op0=ALU.mult, op1=ALU.mult)))(),
            reads=[osb_res[c], "ps_S1"], writes=[f"{pfx}_xtmp"], dbg=f"{pfx}ea{tt}_{c}")
        P.emit("dve", (lambda c=c, B2=B2: (lambda e: e.tensor_tensor(
            xtmp[:, :], xtmp[:, :], B2, op=ALU.add)))(),
            reads=[f"{pfx}_xtmp", "ps_S2"], writes=[f"{pfx}_xtmp"], dbg=f"{pfx}eb{tt}_{c}")
        P.emit("dve", (lambda c=c: (lambda e: e.tensor_tensor(
            xt[:, ts(c, TT)], xt[:, ts(c, TT)], xtmp[:, :], op=ALU.add)))(),
            reads=[f"{pfx}_xtmp"], writes=[f"{pfx}_xt{c}"], dbg=f"{pfx}ec{tt}_{c}")
    P.emit("sp", (lambda tt=tt: (lambda e: e.dma_start(
        fm3(x_out, tt * TT, TT), cm3(xt, NCH * TT, TT))))(),
        reads=[f"{pfx}_xt{c}" for c in range(8)], kind="dma", dbg=f"{pfx}sx{tt}")


def phase_l0_ffn(cx, x_in, x_out, hp):
    nc, P, t = cx.nc, cx.P, cx.t
    PS = t["psum"]
    cx.phase_reset()
    xt = cx.sbp("f0_xt", [128, NCH * TT], F32)
    xnbf = cx.sbp("f0_xnbf", [128, NCH * TT], BF16)
    xtmp = cx.sbp("f0_xtmp", [128, TT], F32)
    rail1 = cx.sbp("f0_rail1", [128, 4 * 8 * 128], BF16)
    rail2 = cx.sbp("f0_rail2", [128, 4 * 32 * 128], BF16)

    def ldw(dst, dres, src_ap, dbg):
        P.emit("pool", lambda e: e.dma_start(dst, src_ap), writes=[dres],
               kind="dma", dbg=dbg)

    ldw(rail1[:, :], "f0_rail1", t["l0_rail1"].ap(), "ld_rail1")
    ldw(rail2[:, :], "f0_rail2", t["l0_rail2"].ap(), "ld_rail2")
    xnsc = t["sc_xn"]
    wcsc = t["sc_wcp"]

    cx.sub_reset()
    xbf = cx.sbs("f0_xbf", [128, NCH * TT], BF16)
    xsq = cx.sbs("f0_xsq", [128, NCH * TT], BF16)
    for tt in range(NTT):
        _preln_tile(cx, "f0", tt, x_in, xt, xbf, xsq, xnbf, xtmp)
        P.emit("sp", (lambda tt=tt: (lambda e: e.dma_start(
            fm3(xnsc, tt * TT, TT), cm3(xnbf, NCH * TT, TT))))(),
            reads=[f"f0_xn{c}" for c in range(8)], writes=[f"f0dr_xn{tt}"],
            kind="dma", dbg=f"f0sxn{tt}")

    P.barrier()
    cx.sub_reset()
    wtmp = [cx.sbs("f0_wtmp0", [128, 8 * 1024], BF16),
            cx.sbs("f0_wtmp1", [128, 8 * 1024], BF16)]
    for s in range(16):
        b = s % 2
        P.emit("pool", (lambda s=s, b=b: (lambda e: e.dma_start(
            cm3(wtmp[b], 8 * 1024, 1024),
            t["l0_wcpT"].ap()[s * 1024:(s + 1) * 1024, :]
            .rearrange("(c p) f -> p c f", p=128))))(),
            writes=[f"f0_wtmp{b}"], kind="dma", dbg=f"f0wl{s}")
        P.emit("sp", (lambda s=s, b=b: (lambda e: e.dma_start(
            wcsc.ap()[s * 1024:(s + 1) * 1024, :]
            .rearrange("(c p) f -> p c f", p=128),
            cm3(wtmp[b], 8 * 1024, 1024))))(),
            reads=[f"f0_wtmp{b}"], writes=[f"f0dr_w{s}"], kind="dma", dbg=f"f0ws{s}")
    wcsc_deps = [f"f0dr_w{s}" for s in range(16)]

    P.barrier()
    cx.sub_reset()
    ybf = cx.sbs("f0_ybf", [128, 8 * TT], BF16)
    zbuf = cx.sbs("f0_zbuf", [128, 16 * TT], BF16)
    wcp0 = cx.sbs("f0_wcp0", [128, 16 * 1024], BF16)
    wcp = [wcp0, wcp0]
    outacc = cx.sbs("f0_outacc", [128, NCH * TT], F32)
    osq = cx.sbs("f0_osq", [128, NCH * TT], BF16)
    obf = cx.sbs("f0_obf", [128, NCH * TT], BF16)

    for tt in range(NTT):
        P.emit("sp", (lambda tt=tt: (lambda e: e.dma_start(
            cm3(xnbf, NCH * TT, TT), fm3(xnsc, tt * TT, TT))))(),
            reads=[f"f0dr_xn{tt}"],
            writes=[f"f0_xn{c}" for c in range(8)], kind="dma", dbg=f"f0lxn{tt}")
        for hd in range(4):
            for yc in range(8):
                psn = f"P{yc % 2}"
                ps_ap = PS[psn][:, 0:TT]
                P.emit("pe", (lambda hd=hd, yc=yc, ps_ap=ps_ap: (lambda e: e.matmul(
                    ps_ap, rail1[:, (hd * 8 + yc) * 128:(hd * 8 + yc) * 128 + 128],
                    xnbf[:, ts(yc, TT)], start=True, stop=True)))(),
                    reads=[f"f0_xn{yc}", "f0_rail1"], writes=[f"ps_{psn}"],
                    kind="mm", dbg=f"f0y{tt}_{hd}_{yc}")
                P.emit("act", (lambda yc=yc, ps_ap=ps_ap: (lambda e: e.activation(
                    ybf[:, ts(yc, TT)], ps_ap, AF.Identity, bias=0.0, scale=1.0)))(),
                    reads=[f"ps_{psn}"], writes=[f"f0_y{yc}"], dbg=f"f0ye{tt}_{hd}_{yc}")
            for zc in range(32):
                g = hd * 32 + zc
                slot = g % 16
                if slot == 0:
                    s = g // 16
                    b = 0
                    P.emit("sp", (lambda s=s, b=b: (lambda e: e.dma_start(
                        cm3(wcp[b], 16 * 1024, 1024, nch=16),
                        wcsc.ap()[s * 2048:(s + 1) * 2048, :]
                        .rearrange("(c p) f -> p c f", p=128))))(),
                        reads=wcsc_deps, writes=[f"f0_wcp{b}"], kind="dma",
                        dbg=f"f0wld{tt}_{s}")
                psn = f"P{2 + zc % 2}"
                ps_ap = PS[psn][:, 0:TT]
                P.emit("pe", (lambda hd=hd, zc=zc, ps_ap=ps_ap: (lambda e: e.matmul(
                    ps_ap, rail2[:, (hd * 32 + zc) * 128:(hd * 32 + zc) * 128 + 128],
                    xnbf[:, ts(zc // 4, TT)] if False else ybf[:, ts(zc // 4, TT)],
                    start=True, stop=True)))(),
                    reads=[f"f0_y{zc // 4}", "f0_rail2"], writes=[f"ps_{psn}"],
                    kind="mm", dbg=f"f0z{tt}_{hd}_{zc}")
                P.emit("act", (lambda slot=slot, ps_ap=ps_ap: (lambda e: e.activation(
                    zbuf[:, ts(slot, TT)], ps_ap, AF.Gelu, bias=0.0, scale=1.0)))(),
                    reads=[f"ps_{psn}"], writes=[f"f0_z{slot}"], dbg=f"f0ze{tt}_{hd}_{zc}")
                if slot == 15:
                    s = g // 16
                    b = 0
                    for oc in range(8):
                        psn2 = "P4" if oc % 2 == 0 else "S1"
                        ps2 = PS[psn2][:, 0:TT]
                        for k in range(16):
                            P.emit("pe", (lambda s=s, b=b, oc=oc, k=k, ps2=ps2:
                                          (lambda e: e.matmul(
                                ps2, wcp[b][:, k * 1024 + oc * 128:
                                            k * 1024 + oc * 128 + 128],
                                zbuf[:, ts(k, TT)],
                                start=(k == 0), stop=(k == 15))))(),
                                reads=[f"f0_z{k}", f"f0_wcp{b}"],
                                writes=[f"ps_{psn2}"] if k in (0, 15) else [],
                                kind="mm", dbg=f"f0c{tt}_{s}_{oc}_{k}")
                        if s == 0:
                            P.emit("dve", (lambda oc=oc, ps2=ps2: (lambda e: e.tensor_copy(
                                outacc[:, ts(oc, TT)], ps2)))(),
                                reads=[f"ps_{psn2}"], writes=[f"f0_oa{oc}"],
                                dbg=f"f0a{tt}_{s}_{oc}")
                        else:
                            P.emit("dve", (lambda oc=oc, ps2=ps2: (lambda e: e.tensor_tensor(
                                outacc[:, ts(oc, TT)], outacc[:, ts(oc, TT)], ps2,
                                op=ALU.add)))(),
                                reads=[f"ps_{psn2}", f"f0_oa{oc}"], writes=[f"f0_oa{oc}"],
                                dbg=f"f0a{tt}_{s}_{oc}")
        _sandwich_epilogue(cx, "f0", tt, outacc, osq, obf, xt, xtmp, x_in, x_out,
                           osb_res=[f"f0_oa{c}" for c in range(8)])




def _rope_apply(cx, pfx, dst_ap, dst_res, stage, stage_res, cossl, sinsl, tmpf,
                rope_ps, dbg, cos_res, sin_res):
    P, t = cx.P, cx.t
    P.emit("pe", lambda e: e.matmul(rope_ps, t["c_ropeP"][:, :], stage,
                                    start=True, stop=True),
           reads=[stage_res, "c_ropeP"], writes=[f"{pfx}_ropeps"], kind="mm",
           dbg=f"{dbg}_rot")
    P.emit("dve", lambda e: e.tensor_tensor(tmpf[:, :], rope_ps, sinsl, op=ALU.mult),
           reads=[f"{pfx}_ropeps", sin_res], writes=[f"{pfx}_ropetmp"],
           dbg=f"{dbg}_t1")
    P.emit("dve", lambda e: e.tensor_tensor(dst_ap, stage, cossl, op=ALU.mult),
           reads=[stage_res, cos_res], writes=[dst_res], dbg=f"{dbg}_t2")
    P.emit("dve", lambda e: e.tensor_tensor(dst_ap, dst_ap, tmpf[:, :], op=ALU.add),
           reads=[f"{pfx}_ropetmp", dst_res], writes=[dst_res], dbg=f"{dbg}_t3")


def phase_l1_attn(cx, x_in, x_out, hp):
    nc, P, t = cx.nc, cx.P, cx.t
    PS = t["psum"]
    ident = t["c_ident"]
    otsc = t["sc_otm"]
    cx.phase_reset()
    cx.sub_reset()
    m0 = cx.sub_mark()

    def ldw(dst, dres, src_ap, dbg):
        P.emit("pool", lambda e: e.dma_start(dst, src_ap), writes=[dres],
               kind="dma", dbg=dbg)

    w1 = cx.sbs("a1_w1", [128, NCH * D], BF16)
    w2 = cx.sbs("a1_w2", [128, NCH * D], BF16)
    w3 = cx.sbs("a1_w3", [128, NCH * D], BF16)
    cosv = cx.sbs("a1_cosv", [128, S], BF16)
    sinv = cx.sbs("a1_sinv", [128, S], BF16)
    xt = cx.sbs("a1_xt", [128, NCH * TT], F32)
    xtmp = cx.sbs("a1_xtmp", [128, TT], F32)
    xbf = cx.sbs("a1_xbf", [128, NCH * TT], BF16)
    xsq = cx.sbs("a1_xsq", [128, NCH * TT], BF16)
    xnbf = cx.sbs("a1_xnbf", [128, NCH * TT], BF16)
    stage = cx.sbs("a1_stage", [128, TT], BF16)
    qvirt = cx.sbs("a1_qvirt", [128, S], BF16)
    kvirt = cx.sbs("a1_kvirt", [128, S], BF16)
    vvirt = cx.sbs("a1_vvirt", [128, S], BF16)
    vvt = cx.sbs("a1_vvt", [128, 32 * 128], BF16)
    rtmp = cx.sbs("a1_rtmp", [128, TT], F32)
    ssb = cx.sbs("a1_ssb", [128, S], F32)
    psb = cx.sbs("a1_psb", [128, S], BF16)
    pkq = cx.sbs("a1_pkq", [128, 8 * 128], BF16)
    otm1 = cx.sbs("a1_otm1", [128, 128], BF16)
    denb = cx.sbs("a1_den", [128, 8], F32)
    recb = cx.sbs("a1_rec", [128, 1], F32)
    mxb = cx.sbs("a1_mx", [128, 8], F32)

    ldw(cm3(w1, NCH * D, D), "a1_w1",
        t["l1_w1T"].ap().rearrange("(c p) f -> p c f", p=128), "a1lw1")
    ldw(cm3(w2, NCH * D, D), "a1_w2",
        t["l1_w2T"].ap().rearrange("(c p) f -> p c f", p=128), "a1lw2")
    ldw(cm3(w3, NCH * D, D), "a1_w3",
        t["l1_w3T"].ap().rearrange("(c p) f -> p c f", p=128), "a1lw3")
    ldw(cosv[:, :], "a1_cosv", t["l1_cos"].ap(), "a1lcos")
    ldw(sinv[:, :], "a1_sinv", t["l1_sin"].ap(), "a1lsin")

    def vstr(buf, fc):
        return bass.AP(buf, fc, [[S, 128], [8, 512]])

    for tt in range(A1_TILES):
        _preln_tile(cx, "a1", tt, x_in, xt, xbf, xsq, xnbf, xtmp)
        xn_deps = [f"a1_xn{c}" for c in range(8)]
        for (wbuf, wres, dst, dres_p, do_rope) in (
                (w1, "a1_w1", qvirt, "a1_qv", True),
                (w2, "a1_w2", kvirt, "a1_kv", True),
                (w3, "a1_w3", vvirt, "a1_vv", False)):
            for fc in range(8):
                psn = f"P{fc % 2}"
                ps_ap = PS[psn][:, 0:TT]
                for c in range(8):
                    P.emit("pe", (lambda wbuf=wbuf, wres=wres, fc=fc, c=c, ps_ap=ps_ap:
                                  (lambda e: e.matmul(
                        ps_ap, wbuf[:, c * D + fc * 128: c * D + fc * 128 + 128],
                        xnbf[:, ts(c, TT)], start=(c == 0), stop=(c == 7))))(),
                        reads=[f"a1_xn{c}", wres],
                        writes=[f"ps_{psn}"] if c in (0, 7) else [], kind="mm",
                        dbg=f"a1p{dres_p}_{tt}_{fc}_{c}")
                if do_rope:
                    scl = t["c_vals"][:, 3:4] if dst is qvirt else 1.0
                    extra = ["c_vals"] if dst is qvirt else []
                    P.emit("act", (lambda fc=fc, ps_ap=ps_ap, scl=scl: (lambda e: e.activation(
                        stage[:, :], ps_ap, AF.Identity, bias=0.0, scale=scl)))(),
                        reads=[f"ps_{psn}"] + extra, writes=["a1_stage"],
                        dbg=f"a1e{dres_p}_{tt}_{fc}")
                    _rope_apply(cx, "a1", vstr(dst, fc), f"{dres_p}{fc}",
                                stage[:, :], "a1_stage",
                                cosv[:, ts(fc, TT)], sinv[:, ts(fc, TT)], rtmp,
                                PS["P2"][:, 0:TT], f"a1r{dres_p}_{tt}_{fc}",
                                "a1_cosv", "a1_sinv")
                else:
                    P.emit("act", (lambda fc=fc, ps_ap=ps_ap, dst=dst: (lambda e: e.activation(
                        vstr(dst, fc), ps_ap, AF.Identity, bias=0.0, scale=1.0)))(),
                        reads=[f"ps_{psn}"], writes=[f"a1_vv{fc}"],
                        dbg=f"a1ev_{tt}_{fc}")
            if not do_rope:
                vdeps = [f"a1_vv{c}" for c in range(8)]
                for sc in range(32):
                    P.emit("pe", (lambda sc=sc: (lambda e: e.transpose(
                        bass.AP(PS["PT"], (sc % 8) * 128, [[1024, 128], [1, 128]]),
                        vvirt[:, sc * 128:(sc + 1) * 128],
                        ident[:, :])))(),
                        reads=vdeps + ["c_ident"],
                        writes=["ps_PT"], kind="mm", dbg=f"a1vt_{tt}_{sc}")
                    if sc % 8 == 7:
                        s0 = (sc // 8) * 8
                        P.emit("dve", (lambda s0=s0: (lambda e: e.tensor_copy(
                            vvt[:, s0 * 128:(s0 + 8) * 128],
                            bass.AP(PS["PT"], 0, [[1024, 128], [1, 1024]]))))(),
                            reads=["ps_PT"], writes=[f"a1_vvt{s0 // 8}"],
                            dbg=f"a1ve_{tt}_{s0}")
        vvt_deps = [f"a1_vvt{i}" for i in range(4)]
        q_deps = [f"a1_qv{c}" for c in range(8)]
        k_deps = [f"a1_kv{c}" for c in range(8)]
        for qs in range(A1_QS):
            for cc in range(8):
                psn = f"P{cc % 2}"
                P.emit("pe", (lambda qs=qs, cc=cc, psn=psn: (lambda e: e.matmul(
                    PS[psn][:, 0:TT], qvirt[:, qs * 128:(qs + 1) * 128],
                    kvirt[:, ts(cc, TT)],
                    start=True, stop=True)))(),
                    reads=q_deps + k_deps, writes=[f"ps_{psn}"],
                    kind="mm", dbg=f"a1s_{tt}_{qs}_{cc}")
                P.emit("act", (lambda cc=cc, psn=psn: (lambda e: e.activation(
                    ssb[:, ts(cc, TT)], PS[psn][:, 0:TT], AF.Identity,
                    bias=0.0, scale=1.0)))(),
                    reads=[f"ps_{psn}"], writes=[f"a1_ssb{cc}"],
                    dbg=f"a1sv_{tt}_{qs}_{cc}")
                P.emit("dve", (lambda cc=cc: (lambda e: e.reduce_max(
                    mxb[:, cc:cc + 1], ssb[:, ts(cc, TT)], axis=AX.X, negate=True)))(),
                    reads=[f"a1_ssb{cc}"], writes=["a1_mxb"], kind="end",
                    dbg=f"a1m_{tt}_{qs}_{cc}")
            if A1_LVL < 2:
                continue
            P.emit("dve", lambda e: e.tensor_reduce(
                recb[:, 0:1], mxb[:, 0:8], axis=AX.X, op=ALU.min),
                reads=["a1_mxb"], writes=["a1_nmx"], kind="end",
                dbg=f"a1mm_{tt}_{qs}")
            P.emit("act", lambda e: e.activation(
                psb[:, :], ssb[:, :], AF.Exp, bias=recb[:, 0:1], scale=1.0,
                accum_out=denb[:, 0:1]),
                reads=[f"a1_ssb{cc}" for cc in range(8)], sreads=["a1_nmx"],
                writes=["a1_psb", "a1_denb"], kind="end", dbg=f"a1e_{tt}_{qs}")
            if A1_LVL < 3:
                continue
            for kc8 in range(4):
                for blk in range(8):
                    P.emit("pe", (lambda kc8=kc8, blk=blk: (lambda e: e.transpose(
                        bass.AP(PS["PT"], blk * 128, [[1024, 128], [1, 128]]),
                        psb[:, (kc8 * 8 + blk) * 128:(kc8 * 8 + blk) * 128 + 128],
                        ident[:, :])))(),
                        reads=["a1_psb", "c_ident"], writes=["ps_PT"],
                        kind="mm", dbg=f"a1pt_{tt}_{qs}_{kc8}_{blk}")
                P.emit("dve", lambda e: e.tensor_copy(
                    pkq[:, :], bass.AP(PS["PT"], 0, [[1024, 128], [1, 1024]])),
                    reads=["ps_PT"], writes=["a1_pkq"], dbg=f"a1pe_{tt}_{qs}_{kc8}")
                for blk in range(8 if A1_LVL >= 4 else 0):
                    kchunk = kc8 * 8 + blk
                    P.emit("pe", (lambda kc8=kc8, blk=blk, kchunk=kchunk:
                                  (lambda e: e.matmul(
                        PS["P3"][:, 0:128], pkq[:, ts(blk, 128)],
                        vvt[:, ts(kchunk, 128)],
                        start=(kchunk == 0), stop=(kchunk == 31))))(),
                        reads=["a1_pkq"] + vvt_deps,
                        writes=["ps_P3"] if kchunk in (0, 31) else [],
                        kind="mm", dbg=f"a1pv_{tt}_{qs}_{kchunk}")
            if A1_LVL < 5:
                continue
            P.emit("dve", lambda e: e.reciprocal(recb[:, 0:1], denb[:, 0:1]),
                   reads=["a1_denb"], writes=["a1_recb"], kind="end",
                   dbg=f"a1r_{tt}_{qs}")
            P.emit("dve", lambda e: e.tensor_scalar(
                otm1[:, :], PS["P3"][:, 0:128], recb[:, 0:1], None, op0=ALU.mult),
                reads=["ps_P3"], sreads=["a1_recb"], writes=["a1_otm1"],
                dbg=f"a1o_{tt}_{qs}")
            P.emit("sp", (lambda tt=tt, qs=qs: (lambda e: e.dma_start(
                otsc.ap()[qs * 128:(qs + 1) * 128, tt * 128:(tt + 1) * 128],
                otm1[:, :])))(),
                reads=["a1_otm1"], writes=[f"a1dr_ot{tt}_{qs}"], kind="dma",
                dbg=f"a1so{tt}_{qs}")
    if DEV_DUMP:
        P.emit("sp", lambda e: e.dma_start(t["d_qbf1"].ap(), qvirt[:, :]),
               reads=[f"a1_qv{c}" for c in range(8)], kind="dma", dbg="dump_qbf1")
    P.barrier()
    cx.sub_release(m0)

    woT = cx.sbs("a1_woT", [128, NCH * D], BF16)
    otmg = cx.sbs("a1_otmg", [128, 8 * D], BF16)
    rlsb = cx.sbs("a1_rlsb", [128, D], F32)
    osb = cx.sbs("a1_osb", [128, NCH * TT], F32)
    osq = cx.sbs("a1_osq", [128, NCH * TT], BF16)
    obf = cx.sbs("a1_obf", [128, NCH * TT], BF16)
    ext = cx.sbs("a1e_xt", [128, NCH * TT], F32)
    extmp = cx.sbs("a1e_xtmp", [128, TT], F32)
    ldw(cm3(woT, NCH * D, D), "a1_woT",
        t["l1_woT"].ap().rearrange("(c p) f -> p c f", p=128), "a1lwo")
    ot_deps = [f"a1dr_ot{tt}_{qs}" for tt in range(A1_TILES) for qs in range(A1_QS)]
    for g in range(4):
        P.emit("sp", (lambda g=g: (lambda e: e.dma_start(
            cm3(otmg, 8 * D, D),
            otsc.ap()[g * 1024:(g + 1) * 1024, :]
            .rearrange("(c p) f -> p c f", p=128))))(),
            reads=ot_deps, writes=["a1_otmg"], kind="dma", dbg=f"a1lg{g}")
        for half in range(2):
            for tc in range(4):
                tauc = half * 4 + tc
                for eh in range(2):
                    psn = f"P{eh}"
                    for pc in range(8):
                        P.emit("pe", (lambda tauc=tauc, eh=eh, pc=pc, psn=psn:
                                      (lambda e: e.matmul(
                            PS[psn][:, 0:TT],
                            woT[:, pc * D + tauc * 128: pc * D + tauc * 128 + 128],
                            otmg[:, pc * D + eh * 512: pc * D + eh * 512 + 512],
                            start=(pc == 0), stop=(pc == 7))))(),
                            reads=["a1_otmg", "a1_woT"],
                            writes=[f"ps_{psn}"] if pc in (0, 7) else [], kind="mm",
                            dbg=f"a1rl_{g}_{tauc}_{eh}")
                    P.emit("dve", (lambda eh=eh, psn=psn: (lambda e: e.tensor_copy(
                        rlsb[:, eh * 512:(eh + 1) * 512], PS[psn][:, 0:TT])))(),
                        reads=[f"ps_{psn}"], writes=["a1_rlsb"], dbg=f"a1re_{g}_{tauc}_{eh}")
                for c in range(8):
                    P.emit("pe", (lambda c=c: (lambda e: e.transpose(
                        bass.AP(PS["S2"], (c % 4) * 128, [[512, 128], [1, 128]]),
                        rlsb[:, ts(c, 128)], t["c_identf"][:, :])))(),
                        reads=["a1_rlsb", "c_identf"],
                        writes=["ps_S2"] if c % 4 in (0, 3) else [], kind="mm",
                        dbg=f"a1rt_{g}_{tauc}_{c}")
                    if c % 4 == 3:
                        c0 = c - 3
                        P.emit("dve", (lambda c0=c0, tc=tc: (lambda e: e.tensor_copy(
                            bass.AP(osb, c0 * TT + tc * 128,
                                    [[NCH * TT, 128], [TT, 4], [1, 128]]),
                            bass.AP(PS["S2"], 0, [[512, 128], [128, 4], [1, 128]]))))(),
                            reads=["ps_S2"], writes=[f"a1_osb{c0}"], dbg=f"a1oe_{g}_{tauc}_{c}")
            tt2 = g * 2 + half
            _sandwich_epilogue(cx, "a1e", tt2, osb, osq, obf, ext, extmp,
                               x_in, x_out,
                               osb_res=["a1_osb0"] * 4 + ["a1_osb4"] * 4)




def phase_l1_ffn(cx, x_in, x_out, hp):
    nc, P, t = cx.nc, cx.P, cx.t
    PS = t["psum"]
    ident = t["c_ident"]
    ya, yb = t["sc_ya"], t["sc_yb"]
    cvals = t["c_vals"]

    def ldw(dst, dres, src_ap, dbg):
        P.emit("pool", lambda e: e.dma_start(dst, src_ap), writes=[dres],
               kind="dma", dbg=dbg)

    cx.phase_reset()
    b1pt = cx.sbp("sb_b1pt", [128, 6 * 44], F32)
    b2pt = cx.sbp("sb_b2pt", [128, 48], F32)
    gpt = cx.sbp("sb_gpt", [128, 48], F32)
    bpt = cx.sbp("sb_bpt", [128, 48], F32)
    qspt = cx.sbp("sb_qspt", [128, 6], F32)
    kspt = cx.sbp("sb_kspt", [128, 6], F32)
    maskb = cx.sbp("sb_maskb", [128, 3 * 384], F32)
    ropeH = cx.sbp("sb_ropeH", [128, 128], BF16)
    ones2 = cx.sbp("sb_ones2", [128, 2], BF16)
    ones2f = cx.sbp("sb_ones2f", [2, 128], F32)
    for nm, buf in (("lf_b1pt", b1pt), ("lf_b2pt", b2pt), ("lf_gpt", gpt),
                    ("lf_bpt", bpt), ("lf_qspt", qspt), ("lf_kspt", kspt),
                    ("lf_maskb", maskb)):
        src_nm = nm if nm != "lf_maskb" else "lf_mask"
        P.emit("sp", (lambda buf=buf, src_nm=src_nm: (lambda e: e.dma_start(
            buf[:, :], t[src_nm].ap())))(), writes=[nm], kind="dma", dbg=f"ld{nm}")
    ldw(ropeH[:, :], "lf_ropeH", t["lf_ropeH"].ap(), "ldropeH")
    ldw(ones2[:, :], "lf_ones2b", t["lf_ones2"].ap(), "ldones2")
    P.emit("sp", lambda e: e.dma_start(ones2f[:, :], t["lf_ones2T"].ap()),
           writes=["lf_ones2f"], kind="dma", dbg="ldones2f")
    cx.sub_reset()
    mbase = cx.sub_mark()

    TH = TT + 8
    TB = 8

    def load_ytile(yt, si, tt, cur, pfx):
        rd = [f"lfdr_{si - 1}_{tt}"] + ([f"lfdr_{si - 1}_{tt - 1}"] if tt else [])
        wres = [f"{pfx}_yt{c}" for c in range(8)]
        P.emit("sp", (lambda yt=yt, cur=cur, tt=tt: (lambda e: e.dma_start(
            bass.AP(yt, TB, [[8 * TH, 128], [TH, 8], [1, TT]]),
            fm3(cur, tt * TT, TT))))(),
            reads=rd, writes=wres, kind="dma", dbg=f"{pfx}ld{si}_{tt}")
        if tt == 0:
            P.emit("pool", (lambda yt=yt: (lambda e: e.memset(
                bass.AP(yt, TB - 1, [[8 * TH, 128], [TH, 8], [1, 1]]), 0.0)))(),
                writes=wres, kind="end", dbg=f"{pfx}z{si}_{tt}")
        else:
            P.emit("sp", (lambda yt=yt, cur=cur, tt=tt: (lambda e: e.dma_start(
                bass.AP(yt, TB - 1, [[8 * TH, 128], [TH, 8], [1, 1]]),
                fm3(cur, tt * TT - 1, 1),
                allow_slow_non_contiguous=True)))(),
                reads=rd, writes=wres, kind="dma", dbg=f"{pfx}lh{si}_{tt}")

    def shift_ap(yt, c, w=TT):
        off = c * TH + (TB - 1 if c >= 4 else TB)
        return bass.AP(yt, off, [[8 * TH, 128], [1, w]])

    def store_ytile(yt, si, tt, nxt, pfx, rres):
        P.emit("sp", (lambda yt=yt, nxt=nxt, tt=tt: (lambda e: e.dma_start(
            fm3(nxt, tt * TT, TT),
            bass.AP(yt, TB, [[8 * TH, 128], [TH, 8], [1, TT]]))))(),
            reads=rres, writes=[f"lfdr_{si}_{tt}"], kind="dma",
            dbg=f"{pfx}st{si}_{tt}")

    m0 = cx.sub_mark()
    xt = cx.sbs("lfp_xt", [128, NCH * TT], F32)
    xtmp = cx.sbs("lfp_xtmp", [128, TT], F32)
    xbf = cx.sbs("lfp_xbf", [128, NCH * TT], BF16)
    xsq = cx.sbs("lfp_xsq", [128, NCH * TT], BF16)
    for tt in range(NTT):
        _preln_tile(cx, "lfp", tt, x_in, xt, xbf, xsq, None, xtmp, apply=False)
        B1 = PS["S1"][:, 0:TT]
        B2 = PS["S2"][:, 0:TT]
        for c in range(8):
            P.emit("dve", (lambda c=c, B1=B1: (lambda e: e.scalar_tensor_tensor(
                xtmp[:, :], xt[:, ts(c, TT)], 1.0, B1, op0=ALU.mult, op1=ALU.mult)))(),
                reads=[f"lfp_xt{c}", "ps_S1"], writes=["lfp_xtmp"], dbg=f"lfpa{tt}_{c}")
            P.emit("dve", (lambda c=c, B2=B2: (lambda e: e.tensor_tensor(
                xt[:, ts(c, TT)], xtmp[:, :], B2, op=ALU.add)))(),
                reads=["lfp_xtmp", "ps_S2"], writes=[f"lfp_xt{c}"], dbg=f"lfpb{tt}_{c}")
        P.emit("sp", (lambda tt=tt: (lambda e: e.dma_start(
            fm3(ya, tt * TT, TT), cm3(xt, NCH * TT, TT))))(),
            reads=[f"lfp_xt{c}" for c in range(8)], writes=[f"lfdr_0_{tt}"],
            kind="dma", dbg=f"lfpst{tt}")
    P.barrier()
    cx.sub_release(m0)

    def _attn_sub(li):
        si = 2 * li + 1
        cur, nxt = (ya, yb) if si % 2 == 1 else (yb, ya)
        if "redirect" in LF_SKIP and si == 1:
            nxt = t["sc_guard"]
        m1 = cx.sub_mark()
        wqkv = cx.sbs(f"at{li}_wqkv", [128, NCH * 3072], BF16)
        wo = cx.sbs(f"at{li}_wo", [128, NCH * D], BF16)
        yt = cx.sbs(f"at{li}_yt", [128, NCH * TH], F32)
        xsbf = cx.sbs(f"at{li}_xs", [128, NCH * TT], BF16)
        stage = cx.sbs(f"at{li}_stage", [128, TT], BF16)
        sqc = cx.sbs(f"at{li}_sqc", [128, TT], BF16)
        rn0 = cx.sbs(f"at{li}_rn0", [2, TT], F32)
        rnrow = cx.sbs(f"at{li}_rn", [2, TT], F32)
        qp = cx.sbs(f"at{li}_qp", [128, NCH * TT], BF16)
        kpb = [cx.sbs(f"at{li}_kp0", [128, NCH * TT], BF16),
               cx.sbs(f"at{li}_kp1", [128, NCH * TT], BF16)]
        vstage = cx.sbs(f"at{li}_vst", [128, NCH * TT], BF16)
        vvtb = [cx.sbs(f"at{li}_vv0", [128, 4 * D], BF16),
                cx.sbs(f"at{li}_vv1", [128, 4 * D], BF16)]
        xpc = [cx.sbs(f"at{li}_xp{j}", [128, TT], BF16) for j in range(4)]
        rtmp = cx.sbs(f"at{li}_rtmp", [128, TT], F32)
        psbL = cx.sbs(f"at{li}_psb", [128, 384], BF16)
        pkqL = cx.sbs(f"at{li}_pkq", [128, 384], BF16)
        obuf = cx.sbs(f"at{li}_ob", [128, NCH * TT], BF16)
        mx1 = cx.sbs(f"at{li}_mx", [128, 2], F32)
        dn1 = cx.sbs(f"at{li}_dn", [128, 2], F32)
        ldw(cm3(wqkv, NCH * 3072, 3072), f"at{li}_wqkv",
            t[f"lf_wqkv{li}"].ap().rearrange("(c p) f -> p c f", p=128), f"at{li}lw")
        ldw(cm3(wo, NCH * D, D), f"at{li}_wo",
            t[f"lf_wo{li}"].ap().rearrange("(c p) f -> p c f", p=128), f"at{li}lo")
        for b_, nm in ((kpb[1], f"at{li}_kp1"), (vvtb[1], f"at{li}_vv1")):
            P.emit("pool", (lambda b_=b_: (lambda e: e.memset(b_[:, :], 0.0)))(),
                   writes=[nm], kind="end", dbg=f"at{li}ms{nm}")
        for tt in range(NTT):
            cb_i = tt % 2
            pb_i = 1 - cb_i
            kp_c, kp_p = kpb[cb_i], kpb[pb_i]
            vv_c, vv_p = vvtb[cb_i], vvtb[pb_i]
            kpc_res, kpp_res = f"at{li}_kp{cb_i}", f"at{li}_kp{pb_i}"
            vvc_res, vvp_res = f"at{li}_vv{cb_i}", f"at{li}_vv{pb_i}"
            load_ytile(yt, si, tt, cur, f"at{li}")
            yres = [f"at{li}_yt{c}" for c in range(8)]
            for c in range(8):
                P.emit("dve", (lambda c=c, yt=yt: (lambda e: e.tensor_copy(
                    xsbf[:, ts(c, TT)], shift_ap(yt, c))))(),
                    reads=yres, writes=[f"at{li}_xs{c}"], dbg=f"at{li}cx{tt}_{c}")
            for j, nm in enumerate(("xp_cosS", "xp_sinS", "xp_cosI", "xp_sinI")):
                ldw(xpc[j][:, :], f"at{li}_xp{j}",
                    t[nm].ap()[:, ts(tt, TT)], f"at{li}xp{tt}_{j}")
            for fc in range(24):
                psn = ("P0", "P1", "P3", "P4")[fc % 4]
                ps_ap = PS[psn][:, 0:TT]
                for c in range(8):
                    P.emit("pe", (lambda fc=fc, c=c, ps_ap=ps_ap: (lambda e: e.matmul(
                        ps_ap, wqkv[:, c * 3072 + fc * 128: c * 3072 + fc * 128 + 128],
                        xsbf[:, ts(c, TT)], start=(c == 0), stop=(c == 7))))(),
                        reads=[f"at{li}_xs{c}", f"at{li}_wqkv"],
                        writes=[f"ps_{psn}"] if c in (0, 7) else [], kind="mm",
                        dbg=f"at{li}qkv{tt}_{fc}_{c}")
                kind3 = fc // 8
                hc = fc % 8
                if kind3 < 2:
                    P.emit("act", (lambda ps_ap=ps_ap: (lambda e: e.activation(
                        stage[:, :], ps_ap, AF.Identity, bias=0.0, scale=1.0)))(),
                        reads=[f"ps_{psn}"], writes=[f"at{li}_stage"],
                        dbg=f"at{li}ev{tt}_{fc}")
                    P.emit("act", lambda e: e.activation(
                        sqc[:, :], stage[:, :], AF.Square, bias=0.0, scale=1.0),
                        reads=[f"at{li}_stage"], writes=[f"at{li}_sqc"],
                        dbg=f"at{li}sq{tt}_{fc}")
                    P.emit("pe", lambda e: e.matmul(
                        PS["S1"][0:2, 0:TT], ones2[:, :], sqc[:, :],
                        start=True, stop=True),
                        reads=[f"at{li}_sqc", "lf_ones2b"], writes=["ps_S1"],
                        kind="mm", dbg=f"at{li}ss{tt}_{fc}")
                    P.emit("act", lambda e: e.activation(
                        rn0[:, :], PS["S1"][0:2, 0:TT], AF.Ln,
                        bias=cvals[0:2, 4:5], scale=1.0),
                        reads=["ps_S1", "c_vals"], writes=[f"at{li}_rn0"],
                        kind="end", dbg=f"at{li}ln{tt}_{fc}")
                    P.emit("act", lambda e: e.activation(
                        rnrow[:, :], rn0[:, :], AF.Exp, bias=0.0,
                        scale=cvals[0:2, 1:2]),
                        reads=[f"at{li}_rn0", "c_vals"], writes=[f"at{li}_rn"],
                        kind="end", dbg=f"at{li}ex{tt}_{fc}")
                    P.emit("pe", lambda e: e.matmul(
                        PS["S2"][:, 0:TT], ones2f[:, :], rnrow[:, :],
                        start=True, stop=True),
                        reads=[f"at{li}_rn", "lf_ones2f"], writes=["ps_S2"],
                        kind="mm", dbg=f"at{li}bc{tt}_{fc}")
                    spt = qspt if kind3 == 0 else kspt
                    spt_res = "lf_qspt" if kind3 == 0 else "lf_kspt"
                    P.emit("dve", (lambda spt=spt, li=li: (lambda e: e.scalar_tensor_tensor(
                        stage[:, :], stage[:, :], spt[:, li:li + 1],
                        PS["S2"][:, 0:TT], op0=ALU.mult, op1=ALU.mult)))(),
                        reads=[f"at{li}_stage", "ps_S2", spt_res],
                        sreads=[spt_res], writes=[f"at{li}_stage"],
                        dbg=f"at{li}nn{tt}_{fc}")
                    dst = qp if kind3 == 0 else kp_c
                    dres = f"at{li}_qp{hc}" if kind3 == 0 else kpc_res
                    cj = (0, 1) if kind3 == 0 else (2, 3)
                    P.emit("pe", lambda e: e.matmul(
                        PS["P2"][:, 0:TT], ropeH[:, :], stage[:, :],
                        start=True, stop=True),
                        reads=[f"at{li}_stage", "lf_ropeH"], writes=["ps_P2"],
                        kind="mm", dbg=f"at{li}ro{tt}_{fc}")
                    P.emit("dve", (lambda cj=cj: (lambda e: e.tensor_tensor(
                        rtmp[:, :], PS["P2"][:, 0:TT], xpc[cj[1]][:, :], op=ALU.mult)))(),
                        reads=["ps_P2", f"at{li}_xp{cj[1]}"], writes=[f"at{li}_rtmp"],
                        dbg=f"at{li}r1{tt}_{fc}")
                    P.emit("dve", (lambda dst=dst, hc=hc, cj=cj: (lambda e: e.tensor_tensor(
                        dst[:, ts(hc, TT)], stage[:, :], xpc[cj[0]][:, :], op=ALU.mult)))(),
                        reads=[f"at{li}_stage", f"at{li}_xp{cj[0]}"], writes=[dres],
                        dbg=f"at{li}r2{tt}_{fc}")
                    P.emit("dve", (lambda dst=dst, hc=hc: (lambda e: e.tensor_tensor(
                        dst[:, ts(hc, TT)], dst[:, ts(hc, TT)], rtmp[:, :], op=ALU.add)))(),
                        reads=[f"at{li}_rtmp"], writes=[dres], dbg=f"at{li}r3{tt}_{fc}")
                else:
                    P.emit("act", (lambda hc=hc, ps_ap=ps_ap: (lambda e: e.activation(
                        vstage[:, ts(hc, TT)], ps_ap, AF.Identity, bias=0.0, scale=1.0)))(),
                        reads=[f"ps_{psn}"], writes=[f"at{li}_vst{hc}"],
                        dbg=f"at{li}ev{tt}_{fc}")
            vst_res = [f"at{li}_vst{c}" for c in range(8)]
            for cb in range(4):
                for fc in range(8):
                    P.emit("pe", (lambda cb=cb, fc=fc: (lambda e: e.transpose(
                        bass.AP(PS["PT"], fc * 128, [[1024, 128], [1, 128]]),
                        vstage[:, fc * TT + cb * 128: fc * TT + cb * 128 + 128],
                        ident[:, :])))(),
                        reads=vst_res + ["c_ident"], writes=["ps_PT"],
                        kind="mm", dbg=f"at{li}vt{tt}_{cb}_{fc}")
                P.emit("dve", (lambda cb=cb, vv_c=vv_c: (lambda e: e.tensor_copy(
                    vv_c[:, ts(cb, D)], bass.AP(PS["PT"], 0, [[1024, 128], [1, 1024]]))))(),
                    reads=["ps_PT"], writes=[vvc_res], dbg=f"at{li}ve{tt}_{cb}")
            q_res = [f"at{li}_qp{c}" for c in range(8)]
            for hp in range(8):
                pvb = "P3" if hp % 2 == 0 else "P4"
                for hh2 in range(2):
                    h = 2 * hp + hh2
                    hrow = 64 * hh2
                    for qs in range(4):
                        psn = f"P{qs % 2}"
                        segs = []
                        w0 = qs * 128 - 256
                        pos = 0
                        while pos < 384:
                            kl = w0 + pos
                            if kl < 0:
                                n = min(-kl, 384 - pos)
                                segs.append((kp_p, kpp_res, vv_p, 512 + kl, n))
                            else:
                                n = 384 - pos
                                segs.append((kp_c, kpc_res, vv_c, kl, n))
                            pos += n
                        off = 0
                        for (kb, kres, vb, kloc, n) in segs:
                            P.emit("pe", (lambda kb=kb, kres=kres, kloc=kloc, n=n,
                                          hp=hp, hrow=hrow, qs=qs, psn=psn, off=off:
                                          (lambda e: e.matmul(
                                PS[psn][:, off:off + n],
                                qp[hrow:hrow + 64,
                                   hp * TT + qs * 128: hp * TT + qs * 128 + 128],
                                kb[hrow:hrow + 64, hp * TT + kloc: hp * TT + kloc + n],
                                start=True, stop=True)))(),
                                reads=[f"at{li}_qp{hp}", kres],
                                writes=[f"ps_{psn}"], kind="mm",
                                dbg=f"at{li}s{tt}_{h}_{qs}")
                            off += n
                        mi = 0
                        if tt == 0 and qs == 0:
                            mi = 1
                        elif tt == 0 and qs == 1:
                            mi = 2
                        P.emit("dve", (lambda psn=psn, mi=mi: (lambda e: e.tensor_tensor(
                            PS[psn][:, 0:384], PS[psn][:, 0:384],
                            maskb[:, mi * 384:(mi + 1) * 384], op=ALU.add)))(),
                            reads=[f"ps_{psn}", "lf_maskb"], writes=[f"ps_{psn}"],
                            dbg=f"at{li}mk{tt}_{h}_{qs}")
                        P.emit("dve", (lambda psn=psn: (lambda e: e.reduce_max(
                            mx1[:, 0:1], PS[psn][:, 0:384], axis=AX.X, negate=True)))(),
                            reads=[f"ps_{psn}"], writes=[f"at{li}_nmx"], kind="end",
                            dbg=f"at{li}mx{tt}_{h}_{qs}")
                        P.emit("dve", lambda e: e.tensor_scalar(
                            mx1[:, 1:2], mx1[:, 0:1], 8.0, None, op0=ALU.mult),
                            reads=[f"at{li}_nmx"], writes=[f"at{li}_nmx8"],
                            kind="end", dbg=f"at{li}m8{tt}_{h}_{qs}")
                        P.emit("act", (lambda psn=psn: (lambda e: e.activation(
                            psbL[:, :], PS[psn][:, 0:384], AF.Exp,
                            bias=mx1[:, 1:2], scale=cvals[:, 5:6],
                            accum_out=dn1[:, 0:1])))(),
                            reads=[f"ps_{psn}", "c_vals"], sreads=[f"at{li}_nmx8"],
                            writes=[f"at{li}_psbL", f"at{li}_dn"], kind="end",
                            dbg=f"at{li}ex{tt}_{h}_{qs}")
                        P.emit("dve", lambda e: e.reciprocal(dn1[:, 1:2], dn1[:, 0:1]),
                               reads=[f"at{li}_dn"], writes=[f"at{li}_rc"],
                               kind="end", dbg=f"at{li}rc{tt}_{h}_{qs}")
                        P.emit("dve", lambda e: e.tensor_scalar(
                            psbL[:, :], psbL[:, :], dn1[:, 1:2], None, op0=ALU.mult),
                            reads=[f"at{li}_psbL"], sreads=[f"at{li}_rc"],
                            writes=[f"at{li}_psbL"], dbg=f"at{li}pn{tt}_{h}_{qs}")
                        for blk in range(3):
                            P.emit("pe", (lambda blk=blk: (lambda e: e.transpose(
                                bass.AP(PS["PT"], blk * 128, [[1024, 128], [1, 128]]),
                                psbL[:, ts(blk, 128)], ident[:, :])))(),
                                reads=[f"at{li}_psbL", "c_ident"], writes=["ps_PT"],
                                kind="mm", dbg=f"at{li}pt{tt}_{h}_{qs}_{blk}")
                        P.emit("dve", lambda e: e.tensor_copy(
                            pkqL[:, :], bass.AP(PS["PT"], 0, [[1024, 128], [1, 384]])),
                            reads=["ps_PT"], writes=[f"at{li}_pkqL"],
                            dbg=f"at{li}pe{tt}_{h}_{qs}")
                        off = 0
                        nblk = 0
                        for (kb, kres, vb, kloc, n) in segs:
                            vres = vvc_res if vb is vv_c else vvp_res
                            for bi in range(n // 128):
                                kc = (kloc // 128) + bi
                                P.emit("pe", (lambda vb=vb, kc=kc, h=h, nblk=nblk,
                                              hrow=hrow, hh2=hh2, qs=qs, pvb=pvb:
                                              (lambda e: e.matmul(
                                    PS[pvb][hrow:hrow + 64, qs * 128:qs * 128 + 128],
                                    vb[:, kc * D + h * 64: kc * D + h * 64 + 64],
                                    pkqL[:, ts(nblk, 128)],
                                    start=(nblk == 0), stop=(nblk == 2))))(),
                                    reads=[f"at{li}_pkqL", vres],
                                    writes=[f"ps_{pvb}"] if nblk in (0, 2) else [],
                                    kind="mm", dbg=f"at{li}pv{tt}_{h}_{qs}_{nblk}")
                                nblk += 1
                P.emit("dve", (lambda hp=hp, pvb=pvb: (lambda e: e.tensor_copy(
                    obuf[:, ts(hp, TT)], PS[pvb][:, 0:TT])))(),
                    reads=[f"ps_{pvb}"], writes=[f"at{li}_ob{hp}"],
                    dbg=f"at{li}oe{tt}_{hp}")
            ob_res = [f"at{li}_ob{c}" for c in range(8)]
            for oc in range(8):
                psn = f"P{oc % 2}"
                ps_ap = PS[psn][:, 0:TT]
                for c in range(8):
                    P.emit("pe", (lambda oc=oc, c=c, ps_ap=ps_ap: (lambda e: e.matmul(
                        ps_ap, wo[:, c * D + oc * 128: c * D + oc * 128 + 128],
                        obuf[:, ts(c, TT)], start=(c == 0), stop=(c == 7))))(),
                        reads=ob_res + [f"at{li}_wo"],
                        writes=[f"ps_{psn}"] if c in (0, 7) else [], kind="mm",
                        dbg=f"at{li}wo{tt}_{oc}_{c}")
                P.emit("dve", (lambda oc=oc, ps_ap=ps_ap, yt=yt: (lambda e: e.tensor_tensor(
                    bass.AP(yt, oc * TH + TB, [[8 * TH, 128], [1, TT]]),
                    bass.AP(yt, oc * TH + TB, [[8 * TH, 128], [1, TT]]),
                    ps_ap, op=ALU.add)))(),
                    reads=[f"ps_{psn}"] + yres, writes=[f"at{li}_yt{oc}"],
                    dbg=f"at{li}rs{tt}_{oc}")
            store_ytile(yt, si, tt, nxt, f"at{li}", yres)
        if DEV_DUMP and li == 0:
            P.emit("pool", (lambda yt=yt: (lambda e: e.dma_start(
                t["d_xn"].ap(), bass.AP(yt, TB, [[8 * TH, 128], [TH, 8], [1, TT]]))))(),
                reads=[f"at{li}_yt{c}" for c in range(8)], kind="dma", dbg="dump_yt7")
        P.barrier()
        cx.sub_release(m1)


    def _ffn_sub(li):
        si = 2 * li + 2
        cur, nxt = (ya, yb) if si % 2 == 1 else (yb, ya)
        m1 = cx.sub_mark()
        w1p = cx.sbs(f"ff{li}_w1p", [128, NCH * 5632], BF16)
        wsl = [cx.sbs(f"ff{li}_wsl0", [128, 22 * 128], BF16),
               cx.sbs(f"ff{li}_wsl1", [128, 22 * 128], BF16)]
        hbuf = cx.sbs(f"ff{li}_h", [128, 22 * TT], BF16)
        yt = cx.sbs(f"ff{li}_yt", [128, NCH * TH], F32)
        xsbf = cx.sbs(f"ff{li}_xs", [128, NCH * TT], BF16)
        sqc = cx.sbs(f"ff{li}_sqc", [128, TT], BF16)
        stage = cx.sbs(f"ff{li}_stage", [128, TT], BF16)
        bb = cx.sbs(f"ff{li}_bb", [128, TT], F32)
        rtmp = cx.sbs(f"ff{li}_rtmp", [128, TT], F32)
        if "w1ld" not in LF_SKIP:
            ldw(cm3(w1p, NCH * 5632, 5632), f"ff{li}_w1p",
                t[f"lf_w1p{li}"].ap().rearrange("(c p) f -> p c f", p=128), f"ff{li}lw1")
        w2f = t["sc_w2f"]
        for s_ in range(2 if "w2cast" not in LF_SKIP else 0):
            P.emit("pool", (lambda s_=s_: (lambda e: e.dma_start(
                cm3(hbuf, 22 * TT, 1024, nch=11),
                t[f"lf_w2p{li}"].ap()[s_ * 1408:(s_ + 1) * 1408, :]
                .rearrange("(c p) f -> p c f", p=128))))(),
                writes=[f"ff{li}_h{fc}" for fc in range(22)], kind="dma",
                dbg=f"ff{li}w2l{s_}")
            for oc in range(8):
                P.emit("sp", (lambda s_=s_, oc=oc: (lambda e: e.dma_start(
                    w2f.ap()[oc * 128:(oc + 1) * 128, s_ * 1408:(s_ + 1) * 1408]
                    .rearrange("p (c f) -> p c f", f=128),
                    bass.AP(hbuf, oc * 128,
                            [[22 * TT, 128], [1024, 11], [1, 128]]))))(),
                    reads=[f"ff{li}_h{fc}" for fc in range(22)],
                    writes=[f"ffdr{li}_w2_{oc}"], kind="dma",
                    dbg=f"ff{li}w2s{s_}_{oc}")
        P.barrier()
        for tt in range(NTT):
            if "ffld" in LF_SKIP:
                break
            load_ytile(yt, si, tt, cur, f"ff{li}")
            yres = [f"ff{li}_yt{c}" for c in range(8)]
            for c in range(8):
                P.emit("dve", (lambda c=c, yt=yt: (lambda e: e.tensor_copy(
                    xsbf[:, ts(c, TT)], shift_ap(yt, c))))(),
                    reads=yres, writes=[f"ff{li}_xs{c}"], dbg=f"ff{li}cx{tt}_{c}")
            if DEV_DUMP and li == 0 and tt == 0:
                P.emit("sp", lambda e: e.dma_start(t["d_xs"].ap(), xsbf[:, :]),
                       reads=[f"ff{li}_xs{c}" for c in range(8)], kind="dma",
                       dbg="dump_xs")
            for c in range(8):
                P.emit("pe", (lambda c=c: (lambda e: e.matmul(
                    PS["S1"][0:1, 0:TT], t["c_ones_bf"][:, 0:1], xsbf[:, ts(c, TT)],
                    start=(c == 0), stop=(c == 7))))(),
                    reads=[f"ff{li}_xs{c}", "c_ones"],
                    writes=["ps_S1"] if c in (0, 7) else [], kind="mm",
                    dbg=f"ff{li}s1{tt}_{c}")
            for c in range(8):
                P.emit("act", (lambda c=c: (lambda e: e.activation(
                    sqc[:, :], xsbf[:, ts(c, TT)], AF.Square, bias=0.0, scale=1.0)))(),
                    reads=[f"ff{li}_xs{c}"], writes=[f"ff{li}_sqc"],
                    dbg=f"ff{li}sq{tt}_{c}")
                P.emit("pe", (lambda c=c: (lambda e: e.matmul(
                    PS["S2"][0:1, 0:TT], t["c_ones_bf"][:, 0:1], sqc[:, :],
                    start=(c == 0), stop=(c == 7))))(),
                    reads=[f"ff{li}_sqc", "c_ones"],
                    writes=["ps_S2"] if c in (0, 7) else [], kind="mm",
                    dbg=f"ff{li}s2{tt}_{c}")
            _ln_finish(cx, f"ff{li}F{tt}", TT)
            B1 = PS["S1"][:, 0:TT]
            B2 = PS["S2"][:, 0:TT]
            for c in range(8):
                P.emit("dve", (lambda c=c, B2=B2, li=li: (lambda e: e.tensor_scalar(
                    bb[:, :], B2, gpt[:, li * 8 + c:li * 8 + c + 1],
                    bpt[:, li * 8 + c:li * 8 + c + 1], op0=ALU.mult, op1=ALU.add)))(),
                    reads=["ps_S2", "lf_gpt", "lf_bpt"],
                    sreads=["lf_gpt", "lf_bpt"], writes=[f"ff{li}_bb"],
                    dbg=f"ff{li}ba{tt}_{c}")
                P.emit("dve", (lambda c=c, B1=B1, yt=yt: (lambda e: e.tensor_tensor(
                    rtmp[:, :], shift_ap(yt, c), B1, op=ALU.mult)))(),
                    reads=yres + ["ps_S1"], writes=[f"ff{li}_rtmp"],
                    dbg=f"ff{li}v1{tt}_{c}")
                P.emit("dve", (lambda c=c, li=li: (lambda e: e.scalar_tensor_tensor(
                    xsbf[:, ts(c, TT)], rtmp[:, :],
                    gpt[:, li * 8 + c:li * 8 + c + 1], bb[:, :],
                    op0=ALU.mult, op1=ALU.add)))(),
                    reads=[f"ff{li}_rtmp", f"ff{li}_bb", "lf_gpt"],
                    sreads=["lf_gpt"], writes=[f"ff{li}_xs{c}"],
                    dbg=f"ff{li}v2{tt}_{c}")
            xn_res = [f"ff{li}_xs{c}" for c in range(8)]
            if DEV_DUMP and li == 0 and tt == 0:
                P.emit("sp", lambda e: e.dma_start(t["d_xn"].ap(), xsbf[:, :]),
                       reads=xn_res, kind="dma", dbg="dump_xn")
            for fc in range(22 if "w1" not in LF_SKIP else 0):
                pa = ("P0", "P2")[fc % 2]
                pb = ("P1", "P4")[fc % 2]
                for c in range(8):
                    P.emit("pe", (lambda fc=fc, c=c, pa=pa: (lambda e: e.matmul(
                        PS[pa][:, 0:TT],
                        w1p[:, c * 5632 + fc * 128: c * 5632 + fc * 128 + 128],
                        xsbf[:, ts(c, TT)], start=(c == 0), stop=(c == 7))))(),
                        reads=xn_res + [f"ff{li}_w1p"],
                        writes=[f"ps_{pa}"] if c in (0, 7) else [], kind="mm",
                        dbg=f"ff{li}wa{tt}_{fc}_{c}")
                for c in range(8):
                    P.emit("pe", (lambda fc=fc, c=c, pb=pb: (lambda e: e.matmul(
                        PS[pb][:, 0:TT],
                        w1p[:, c * 5632 + 2816 + fc * 128:
                            c * 5632 + 2816 + fc * 128 + 128],
                        xsbf[:, ts(c, TT)], start=(c == 0), stop=(c == 7))))(),
                        reads=xn_res + [f"ff{li}_w1p"],
                        writes=[f"ps_{pb}"] if c in (0, 7) else [], kind="mm",
                        dbg=f"ff{li}wg{tt}_{fc}_{c}")
                P.emit("act", (lambda fc=fc, li=li, pb=pb: (lambda e: e.activation(
                    stage[:, :], PS[pb][:, 0:TT], AF.Gelu,
                    bias=b1pt[:, li * 44 + 22 + fc: li * 44 + 23 + fc], scale=1.0)))(),
                    reads=[f"ps_{pb}", "lf_b1pt"], writes=[f"ff{li}_stage"],
                    dbg=f"ff{li}ge{tt}_{fc}")
                P.emit("dve", (lambda fc=fc, li=li, pa=pa: (lambda e: e.scalar_tensor_tensor(
                    hbuf[:, ts(fc, TT)], PS[pa][:, 0:TT],
                    b1pt[:, li * 44 + fc: li * 44 + fc + 1], stage[:, :],
                    op0=ALU.add, op1=ALU.mult)))(),
                    reads=[f"ps_{pa}", f"ff{li}_stage", "lf_b1pt"],
                    sreads=["lf_b1pt"], writes=[f"ff{li}_h{fc}"],
                    dbg=f"ff{li}hm{tt}_{fc}")
            h_res = [f"ff{li}_h{fc}" for fc in range(22)]
            if DEV_DUMP and li == 0 and tt == 0:
                P.emit("sp", lambda e: e.dma_start(t["d_hb"].ap(), hbuf[:, :]),
                       reads=h_res, kind="dma", dbg="dump_hb")
                P.emit("sp", lambda e: e.dma_start(t["d_w2f"].ap(), w2f.ap()),
                       reads=[f"ffdr{li}_w2_{oc}" for oc in range(8)], kind="dma",
                       dbg="dump_w2f")
            for oc in range(8 if "w2" not in LF_SKIP else 0):
                b_ = oc % 2
                P.emit("sp", (lambda oc=oc, b_=b_: (lambda e: e.dma_start(
                    wsl[b_][:, :], w2f.ap()[oc * 128:(oc + 1) * 128, :])))(),
                    reads=[f"ffdr{li}_w2_{oc}"], writes=[f"ff{li}_wsl{b_}"],
                    kind="dma", dbg=f"ff{li}wld{tt}_{oc}")
                psn = f"P{2 + oc % 2}"
                for fc in range(22):
                    P.emit("pe", (lambda oc=oc, fc=fc, psn=psn, b_=b_: (lambda e: e.matmul(
                        PS[psn][:, 0:TT],
                        wsl[b_][:, fc * 128:(fc + 1) * 128],
                        hbuf[:, ts(fc, TT)], start=(fc == 0), stop=(fc == 21))))(),
                        reads=h_res + [f"ff{li}_wsl{b_}"],
                        writes=[f"ps_{psn}"] if fc in (0, 21) else [], kind="mm",
                        dbg=f"ff{li}w2{tt}_{oc}_{fc}")
                P.emit("dve", (lambda oc=oc, psn=psn, yt=yt, li=li:
                               (lambda e: e.scalar_tensor_tensor(
                    bass.AP(yt, oc * TH + TB, [[8 * TH, 128], [1, TT]]),
                    PS[psn][:, 0:TT], b2pt[:, li * 8 + oc: li * 8 + oc + 1],
                    bass.AP(yt, oc * TH + TB, [[8 * TH, 128], [1, TT]]),
                    op0=ALU.add, op1=ALU.add)))(),
                    reads=[f"ps_{psn}", "lf_b2pt"] + yres,
                    sreads=["lf_b2pt"], writes=[f"ff{li}_yt{oc}"],
                    dbg=f"ff{li}rs{tt}_{oc}")
            if "ffst" not in LF_SKIP:
                store_ytile(yt, si, tt, nxt, f"ff{li}", yres)
        P.barrier()
        cx.sub_release(m1)


    for li in range(6):
        if 2 * li + 1 > LF_SUBS:
            break
        _attn_sub(li)
        if 2 * li + 2 > LF_SUBS:
            break
        _ffn_sub(li)

    if "postup" in LF_SKIP:
        cx.sub_off = max(cx.sub_off, cx.sub_base + 60 * 1024)
    m1 = cx.sub_mark()
    osb = cx.sbs("lfq_osb", [128, NCH * TT], F32)
    osq = cx.sbs("lfq_osq", [128, NCH * TT], BF16)
    obf2 = cx.sbs("lfq_obf", [128, NCH * TT], BF16)
    ext2 = cx.sbs("lfq_xt", [128, NCH * TT], F32)
    extmp2 = cx.sbs("lfq_xtmp", [128, TT], F32)
    ycur = ya
    for tt in range(NTT):
        P.emit("sp", (lambda tt=tt: (lambda e: e.dma_start(
            cm3(osb, NCH * TT, TT), fm3(ycur, tt * TT, TT))))(),
            reads=[f"lfdr_12_{tt}"],
            writes=[f"lfq_osb{c}" for c in range(8)], kind="dma", dbg=f"lfq{tt}")
        _sandwich_epilogue(cx, "lfq", tt, osb, osq, obf2, ext2, extmp2,
                           x_in, x_out,
                           osb_res=[f"lfq_osb{c}" for c in range(8)])
    P.barrier()
    cx.sub_release(mbase)


def phase_l0_attn(cx, x_in, x_out, hp):
    nc, P, t = cx.nc, cx.P, cx.t
    J = hp["_l0_J"]
    prefix, ncol = hp["_l0_prefix"], hp["_l0_ncol"]
    KCH = (J + 127) // 128
    PS = t["psum"]
    ident = t["c_ident"]
    jtiles = [(0, 512), (512, 512), (1024, J - 1 - 1024)]

    cx.phase_reset()
    xt = cx.sbp("l0_xt", [128, NCH * TT], F32)
    xnbf = cx.sbp("l0_xnbf", [128, NCH * TT], BF16)
    xtmp = cx.sbp("l0_xtmp", [128, TT], F32)
    kc = cx.sbp("l0_kc", [128, NCH * J], BF16)
    vtm = cx.sbp("l0_vtm", [128, KCH * D], BF16)
    den = cx.sb("l0_den", [128, 8], F32)
    rec = cx.sb("l0_rec", [128, 8], F32)
    bconv = cx.sb("l0_bconv", [128, 8], F32)
    nullk = cx.sb("l0_nullk", [128, 8], F32)
    nullv = cx.sb("l0_nullv", [128, 8], F32)
    bout = cx.sb("l0_bout", [128, 8], F32)
    band = cx.sb("l0_bandm", [128, 32 * 64], F32)

    def ldw(dst, dres, src_ap, dbg):
        P.emit("pool", lambda e: e.dma_start(dst, src_ap), writes=[dres],
               kind="dma", dbg=dbg)

    kraw, vraw, xnsc = t["sc_kraw"], t["sc_vraw"], t["sc_xn"]
    ldw(bconv[:, :], "l0_bconv", t["l0_bconv_pt"].ap(), "ld_bconv")
    ldw(nullk[:, :], "l0_nullk", t["l0_nullk_pt"].ap(), "ld_nullk")
    ldw(nullv[:, :], "l0_nullv", t["l0_nullv_pt"].ap(), "ld_nullv")
    ldw(bout[:, :], "l0_bout", t["l0_bout_pt"].ap(), "ld_bout")
    ldw(band[:, :], "l0_bandm", t["l0_band"].ap(), "ld_band")

    cx.sub_reset()
    xbf = cx.sbs("l0_xbf", [128, NCH * TT], BF16)
    xsq = cx.sbs("l0_xsq", [128, NCH * TT], BF16)
    kvst = cx.sbs("l0_kvst", [128, NCH * TT], BF16)
    wkv = cx.sbs("l0_wkv", [128, NCH * 2048], BF16)
    ldw(cm3(wkv, NCH * 2048, 2048), "l0_wkv",
        t["l0_wqkvT"].ap()[:, 1024:3072].rearrange("(c p) f -> p c f", p=128), "ld_wkv")
    for tt in range(NTT):
        xsl = fm3(x_in, tt * TT, TT)
        P.emit("sp", (lambda xsl=xsl: (lambda e: e.dma_start(
            cm3(xt, NCH * TT, TT), xsl)))(),
               writes=[f"l0_xt{c}" for c in range(8)], kind="dma", dbg=f"ld_x{tt}")
        for c in range(8):
            P.emit("dve", (lambda c=c: (lambda e: e.tensor_copy(
                xbf[:, ts(c, TT)], xt[:, ts(c, TT)])))(),
                reads=[f"l0_xt{c}"], writes=[f"l0_xbf{c}"], dbg=f"cx_{tt}_{c}")
            P.emit("act", (lambda c=c: (lambda e: e.activation(
                xsq[:, ts(c, TT)], xt[:, ts(c, TT)], AF.Square, bias=0.0, scale=1.0)))(),
                reads=[f"l0_xt{c}"], writes=[f"l0A_sq{c}"], dbg=f"sx_{tt}_{c}")
        B1, B2 = _ln_stats(cx, f"l0A{tt}", "l0_xbf",
                           [xbf[:, ts(c, TT)] for c in range(8)],
                           "l0A_sq", [xsq[:, ts(c, TT)] for c in range(8)], TT)
        for c in range(8):
            P.emit("dve", (lambda c=c, B1=B1: (lambda e: e.scalar_tensor_tensor(
                xtmp[:, :], xt[:, ts(c, TT)], 1.0, B1,
                op0=ALU.mult, op1=ALU.mult)))(),
                reads=[f"l0_xt{c}", "ps_S1"], writes=["l0_xtmp"], dbg=f"lm_{tt}_{c}")
            P.emit("dve", (lambda c=c, B2=B2: (lambda e: e.tensor_tensor(
                xnbf[:, ts(c, TT)], xtmp[:, :], B2, op=ALU.add)))(),
                reads=["l0_xtmp", "ps_S2"], writes=[f"l0_xn{c}"], dbg=f"la_{tt}_{c}")
        P.emit("sp", (lambda tt=tt: (lambda e: e.dma_start(
            fm3(xnsc, tt * TT, TT), cm3(xnbf, NCH * TT, TT))))(),
            reads=[f"l0_xn{c}" for c in range(8)], writes=[f"dr_xn{tt}"],
            kind="dma", dbg=f"st_xn{tt}")
        for kv in range(2):
            dst_sc = kraw if kv == 0 else vraw
            for fc in range(8):
                psn = f"P{fc % 4}"
                ps_ap = PS[psn][:, 0:TT]
                for c in range(8):
                    P.emit("pe", (lambda fc=fc, c=c, kv=kv, ps_ap=ps_ap: (lambda e: e.matmul(
                        ps_ap,
                        wkv[:, c * 2048 + kv * 1024 + fc * 128:
                            c * 2048 + kv * 1024 + fc * 128 + 128],
                        xnbf[:, ts(c, TT)], start=(c == 0), stop=(c == 7))))(),
                        reads=[f"l0_xn{c}", "l0_wkv"],
                        writes=[f"ps_{psn}"] if c in (0, 7) else [], kind="mm",
                        dbg=f"kv{kv}_{tt}_{fc}_{c}")
                P.emit("act", (lambda fc=fc, ps_ap=ps_ap: (lambda e: e.activation(
                    kvst[:, ts(fc, TT)], ps_ap, AF.Identity, bias=0.0, scale=1.0)))(),
                    reads=[f"ps_{psn}"], writes=[f"l0_kvst{fc}"], dbg=f"ke{kv}_{tt}_{fc}")
            P.emit("sp", (lambda dst_sc=dst_sc, tt=tt: (lambda e: e.dma_start(
                fm3(dst_sc, tt * TT, TT), cm3(kvst, NCH * TT, TT))))(),
                reads=[f"l0_kvst{fc}" for fc in range(8)],
                writes=[f"dr_kv{kv}_{tt}"], kind="dma", dbg=f"skv{kv}_{tt}")

    P.barrier()
    cx.sub_reset()
    ktile = cx.sbs("l0_ktile", [128, NCH * 1540], BF16)
    vtile = cx.sbs("l0_vtile", [128, NCH * 1540], BF16)
    vc = cx.sbs("l0_vc", [128, NCH * J], BF16)
    convw = cx.sbs("l0_convw_sb", [128, 3 * 8 * 128], BF16)
    ldw(convw[:, :], "l0_convw_sb", t["l0_convw"].ap(), "ld_convw")
    P.emit("dve", lambda e: e.tensor_copy(
        bass.AP(kc, 0, [[NCH * J, 128], [J, 8]]), nullk[:, 0:8]),
        reads=["l0_nullk"], writes=["l0_kc_null"], dbg="null_k")
    P.emit("dve", lambda e: e.tensor_copy(
        bass.AP(vc, 0, [[NCH * J, 128], [J, 8]]), nullv[:, 0:8]),
        reads=["l0_nullv"], writes=["l0_vc_null"], dbg="null_v")

    for (j0, nj) in jtiles:
        base = 3 * j0 - 2
        lo = max(base, 0)
        width = 3 * nj
        off = lo - base
        for (tile, raw, nm) in ((ktile, kraw, "k"), (vtile, vraw, "v")):
            if off:
                P.emit("pool", (lambda tile=tile: (lambda e: e.memset(
                    bass.AP(tile, 0, [[NCH * 1540, 128], [1540, 8], [1, 2]]), 0.0)))(),
                    writes=[f"l0_{nm}tile"], kind="end", dbg=f"z_{nm}{j0}")
            P.emit("sp", (lambda tile=tile, raw=raw, lo=lo, width=width, off=off:
                          (lambda e: e.dma_start(
                bass.AP(tile, off, [[NCH * 1540, 128], [1540, 8], [1, width - off]]),
                raw.ap()[:, lo:lo + width - off]
                .rearrange("(c p) t -> p c t", p=128))))(),
                reads=[f"dr_kv{0 if nm == 'k' else 1}_{t2}" for t2 in range(NTT)],
                writes=[f"l0_{nm}tile"], kind="dma", dbg=f"lt_{nm}{j0}")
        for (tile, dst, dres, nm) in ((ktile, kc, "l0_kc", "k"),
                                      (vtile, vc, "l0_vc", "v")):
            for oc in range(8):
                psn = f"P{oc % 4}"
                ps_ap = PS[psn][:, 0:nj]
                for tau in range(3):
                    P.emit("pe", (lambda oc=oc, tau=tau, tile=tile, ps_ap=ps_ap, nj=nj:
                                  (lambda e: e.matmul(
                        ps_ap,
                        convw[:, (tau * 8 + oc) * 128:(tau * 8 + oc) * 128 + 128],
                        bass.AP(tile, oc * 1540 + tau, [[NCH * 1540, 128], [3, nj]]),
                        start=(tau == 0), stop=(tau == 2))))(),
                        reads=[f"l0_{nm}tile", "l0_convw_sb"],
                        writes=[f"ps_{psn}"] if tau in (0, 2) else [], kind="mm",
                        dbg=f"cv_{nm}_{j0}_{oc}_{tau}")
                P.emit("act", (lambda oc=oc, dst=dst, j0=j0, nj=nj, ps_ap=ps_ap:
                               (lambda e: e.activation(
                    dst[:, oc * J + 1 + j0: oc * J + 1 + j0 + nj], ps_ap,
                    AF.Identity, bias=bconv[:, oc:oc + 1], scale=1.0)))(),
                    reads=[f"ps_{psn}", "l0_bconv"],
                    writes=[f"{dres}_{j0}_{oc}"], dbg=f"ce_{nm}_{j0}_{oc}")

    kc_deps = ["l0_kc_null"] + [f"l0_kc_{j0}_{oc}" for (j0, _) in jtiles for oc in range(8)]
    vc_deps = ["l0_vc_null"] + [f"l0_vc_{j0}_{oc}" for (j0, _) in jtiles for oc in range(8)]
    for kk in range(KCH):
        kw = min(128, J - kk * 128)
        for c in range(8):
            P.emit("pe", (lambda kk=kk, c=c, kw=kw: (lambda e: e.transpose(
                bass.AP(PS["PT"], c * 128, [[1024, kw], [1, 128]]),
                vc[:, c * J + kk * 128: c * J + kk * 128 + kw], ident[:, :])))(),
                reads=vc_deps + ["c_ident"],
                writes=["ps_PT"] if c in (0, 7) else [], kind="mm", dbg=f"vt_{kk}_{c}")
        P.emit("dve", (lambda kk=kk, kw=kw: (lambda e: e.tensor_copy(
            vtm[0:kw, ts(kk, D)], bass.AP(PS["PT"], 0, [[1024, kw], [1, 1024]]))))(),
            reads=["ps_PT"], writes=[f"l0_vtm{kk}"], dbg=f"ve_{kk}")
    vtm_deps = [f"l0_vtm{kk}" for kk in range(KCH)]

    if DEV_DUMP:
        P.emit("sp", lambda e: e.dma_start(t["d_convw"].ap(), convw[:, :]),
               reads=["l0_convw_sb"], kind="dma", dbg="dump_convw")
        P.emit("sp", lambda e: e.dma_start(t["d_ktile"].ap(), ktile[:, :]),
               reads=["l0_ktile"], kind="dma", dbg="dump_ktile")
        P.emit("sp", lambda e: e.dma_start(t["d_kc"].ap(), kc[:, :]),
               reads=kc_deps, kind="dma", dbg="dump_kc")
        P.emit("sp", lambda e: e.dma_start(t["d_vtm"].ap(), vtm[:, :]),
               reads=vtm_deps, kind="dma", dbg="dump_vtm")

    P.barrier()
    cx.sub_reset()
    wq = cx.sbs("l0_wq", [128, NCH * D], BF16)
    wout = cx.sbs("l0_wout", [128, NCH * D], BF16)
    qbf = cx.sbs("l0_qbf", [128, NCH * TT], BF16)
    psb = cx.sbs("l0_psb", [128, 8 * J], BF16)
    pkq = cx.sbs("l0_pkq", [128, KCH * 128], BF16)
    otm = cx.sbs("l0_otm", [128, D], BF16)
    ofm = cx.sbs("l0_ofm", [128, NCH * TT], BF16)
    osb = cx.sbs("l0_osb", [128, NCH * TT], F32)
    osq = cx.sbs("l0_osq", [128, NCH * TT], BF16)
    obf = cx.sbs("l0_obf", [128, NCH * TT], BF16)
    ldw(cm3(wq, NCH * D, D), "l0_wq",
        t["l0_wqkvT"].ap()[:, 0:1024].rearrange("(c p) f -> p c f", p=128), "ld_wq")
    ldw(cm3(wout, NCH * D, D), "l0_wout",
        t["l0_woutT"].ap().rearrange("(c p) f -> p c f", p=128), "ld_wout")
    SCALE = 1.0 / 32.0
    for tt in range(NTT):
        P.emit("sp", (lambda tt=tt: (lambda e: e.dma_start(
            cm3(xnbf, NCH * TT, TT), fm3(xnsc, tt * TT, TT))))(),
            reads=[f"dr_xn{tt}"],
            writes=[f"l0_xn{c}" for c in range(8)], kind="dma", dbg=f"lxn{tt}")
        for fc in range(8):
            psn = f"P{fc % 4}"
            ps_ap = PS[psn][:, 0:TT]
            for c in range(8):
                P.emit("pe", (lambda fc=fc, c=c, ps_ap=ps_ap: (lambda e: e.matmul(
                    ps_ap, wq[:, c * D + fc * 128: c * D + fc * 128 + 128],
                    xnbf[:, ts(c, TT)], start=(c == 0), stop=(c == 7))))(),
                    reads=[f"l0_xn{c}", "l0_wq"],
                    writes=[f"ps_{psn}"] if c in (0, 7) else [], kind="mm",
                    dbg=f"q_{tt}_{fc}_{c}")
            P.emit("act", (lambda fc=fc, ps_ap=ps_ap: (lambda e: e.activation(
                qbf[:, ts(fc, TT)], ps_ap, AF.Identity, bias=0.0,
                scale=t["c_vals"][:, 2:3])))(),
                reads=[f"ps_{psn}", "c_vals"], writes=[f"l0_q{fc}"], dbg=f"qe_{tt}_{fc}")

        for qs in range(4):
            Q = tt * 4 + qs
            nc_ = ncol[Q]
            pf = prefix[Q]
            bw = nc_ - pf
            nseg = (nc_ + 511) // 512
            nkk = (nc_ + 127) // 128
            mx = t["st_mx"]
            for hg in range(2):
                for hh in range(8):
                    h = hg * 8 + hh
                    fc = h // 2
                    hrow = 64 * (h % 2)
                    segs = []
                    for sgi in range(nseg):
                        c0 = sgi * 512
                        cw = min(512, nc_ - c0)
                        psn = f"P{sgi}"
                        segs.append((psn, c0, cw))
                        P.emit("pe", (lambda fc=fc, hrow=hrow, c0=c0, cw=cw, psn=psn,
                                      qs=qs: (lambda e: e.matmul(
                            PS[psn][:, 0:cw],
                            qbf[hrow:hrow + 64,
                                fc * TT + qs * 128: fc * TT + qs * 128 + 128],
                            kc[hrow:hrow + 64, fc * J + c0: fc * J + c0 + cw],
                            start=True, stop=True)))(),
                            reads=[f"l0_q{fc}"] + kc_deps,
                            writes=[f"ps_{psn}"], kind="mm", dbg=f"s_{Q}_{h}_{sgi}")
                    for (pb, c0b, cwb) in segs:
                        ov0 = max(pf, c0b)
                        ov1 = min(pf + bw, c0b + cwb)
                        if ov0 >= ov1:
                            continue
                        P.emit("dve", (lambda pb=pb, c0b=c0b, ov0=ov0, ov1=ov1, pf=pf, Q=Q:
                                       (lambda e: e.tensor_tensor(
                            PS[pb][:, ov0 - c0b: ov1 - c0b],
                            PS[pb][:, ov0 - c0b: ov1 - c0b],
                            band[:, Q * 64 + ov0 - pf: Q * 64 + ov1 - pf],
                            op=ALU.add)))(),
                            reads=[f"ps_{pb}", "l0_bandm"], writes=[f"ps_{pb}"],
                            dbg=f"m_{Q}_{h}")
                    for sgi, (pan, c0, cw) in enumerate(segs):
                        P.emit("dve", (lambda pan=pan, cw=cw, sgi=sgi: (lambda e: e.reduce_max(
                            mx[:, sgi:sgi + 1], PS[pan][:, 0:cw], axis=AX.X, negate=True)))(),
                            reads=[f"ps_{pan}"], writes=["st_mx"], kind="end",
                            dbg=f"mx_{Q}_{h}_{sgi}")
                    P.emit("dve", (lambda nseg=nseg: (lambda e: e.tensor_reduce(
                        mx[:, 7:8], mx[:, 0:nseg], axis=AX.X, op=ALU.min)))(),
                        reads=["st_mx"], writes=["st_mx2"], kind="end", dbg=f"mm_{Q}_{h}")
                    for sgi, (pan, c0, cw) in enumerate(segs):
                        P.emit("act", (lambda pan=pan, c0=c0, cw=cw, hh=hh, sgi=sgi, mx=mx:
                                       (lambda e: e.activation(
                            psb[:, hh * J + c0: hh * J + c0 + cw], PS[pan][:, 0:cw],
                            AF.Exp, bias=mx[:, 7:8], scale=1.0,
                            accum_out=(den[:, hh:hh + 1] if sgi == 0
                                       else t["st_acc"][:, sgi:sgi + 1]))))(),
                            reads=[f"ps_{pan}"], sreads=["st_mx2"],
                            writes=[f"l0_psb{hh}",
                                    (f"l0_den{hh}" if sgi == 0 else "st_acc")],
                            kind="end", dbg=f"e_{Q}_{h}_{sgi}")
                    for sgi in range(1, nseg):
                        P.emit("dve", (lambda hh=hh, sgi=sgi: (lambda e: e.scalar_tensor_tensor(
                            den[:, hh:hh + 1], t["st_acc"][:, sgi:sgi + 1], 1.0,
                            den[:, hh:hh + 1], op0=ALU.mult, op1=ALU.add)))(),
                            reads=["st_acc", f"l0_den{hh}"], writes=[f"l0_den{hh}"],
                            dbg=f"d_{Q}_{h}_{sgi}")
                P.emit("dve", lambda e: e.reciprocal(rec[:, 0:8], den[:, 0:8]),
                       reads=[f"l0_den{hh}" for hh in range(8)], writes=["l0_rec"],
                       kind="end", dbg=f"r_{Q}_{hg}")
                for hh in range(8):
                    h = hg * 8 + hh
                    P.emit("dve", (lambda hh=hh, nc_=nc_: (lambda e: e.tensor_scalar(
                        psb[:, hh * J: hh * J + nc_], psb[:, hh * J: hh * J + nc_],
                        rec[:, hh:hh + 1], None, op0=ALU.mult)))(),
                        reads=[f"l0_psb{hh}"], sreads=["l0_rec"], writes=[f"l0_psb{hh}"],
                        dbg=f"pn_{Q}_{h}")
                    for kk in range(nkk):
                        kw = min(128, nc_ - kk * 128)
                        P.emit("pe", (lambda hh=hh, kk=kk, kw=kw: (lambda e: e.transpose(
                            bass.AP(PS["PT"], (kk % 8) * 128, [[1024, kw], [1, 128]]),
                            psb[:, hh * J + kk * 128: hh * J + kk * 128 + kw],
                            ident[:, :])))(),
                            reads=[f"l0_psb{hh}", "c_ident"], writes=["ps_PT"],
                            kind="mm", dbg=f"pt_{Q}_{h}_{kk}")
                        if kk % 8 == 7 or kk == nkk - 1:
                            k0 = (kk // 8) * 8
                            ww = (kk - k0 + 1) * 128
                            P.emit("dve", (lambda k0=k0, ww=ww: (lambda e: e.tensor_copy(
                                pkq[:, k0 * 128: k0 * 128 + ww],
                                bass.AP(PS["PT"], 0, [[1024, 128], [1, ww]]))))(),
                                reads=["ps_PT"], writes=["l0_pkq"], dbg=f"pe_{Q}_{h}_{kk}")
                    pvb = "P4"
                    for kk in range(nkk):
                        kw = min(128, nc_ - kk * 128)
                        P.emit("pe", (lambda hh=hh, kk=kk, kw=kw, h=h, pvb=pvb, nkk=nkk:
                                      (lambda e: e.matmul(
                            PS[pvb][:, hh * 64: hh * 64 + 64],
                            pkq[0:kw, ts(kk, 128)],
                            vtm[0:kw, kk * D + h * 64: kk * D + h * 64 + 64],
                            start=(kk == 0), stop=(kk == nkk - 1))))(),
                            reads=["l0_pkq"] + vtm_deps,
                            writes=[f"ps_{pvb}"] if kk in (0, nkk - 1) else [], kind="mm",
                            dbg=f"pv_{Q}_{h}_{kk}")
                pvb = "P4"
                P.emit("dve", (lambda pvb=pvb, hg=hg: (lambda e: e.tensor_copy(
                    otm[:, hg * 512:(hg + 1) * 512], PS[pvb][:, 0:512])))(),
                    reads=[f"ps_{pvb}"], writes=["l0_otm"], dbg=f"oe_{Q}_{hg}")
                if DEV_DUMP and tt == 3 and qs == 2 and hg == 0:
                    P.emit("sp", lambda e: e.dma_start(t["d_pkq"].ap(), pkq[:, :]),
                           reads=["l0_pkq"], kind="dma", dbg="dump_pkq")
                    P.emit("sp", lambda e: e.dma_start(t["d_otm"].ap(), otm[:, :]),
                           reads=["l0_otm"], kind="dma", dbg="dump_otm")
                    P.emit("sp", lambda e: e.dma_start(t["d_psb"].ap(), psb[:, :]),
                           reads=[f"l0_psb{x}" for x in range(8)], kind="dma", dbg="dump_psb")
                    P.emit("sp", lambda e: e.dma_start(t["d_den"].ap()[:, 0:8], den[:, :]),
                           reads=[f"l0_den{x}" for x in range(8)], kind="dma", dbg="dump_den")
                    P.emit("sp", lambda e: e.dma_start(t["d_qbf"].ap(), qbf[:, :]),
                           reads=[f"l0_q{x}" for x in range(8)], kind="dma", dbg="dump_qbf")
            for c in range(8):
                P.emit("pe", (lambda c=c: (lambda e: e.transpose(
                    bass.AP(PS["PT"], c * 128, [[1024, 128], [1, 128]]),
                    otm[:, ts(c, 128)], ident[:, :])))(),
                    reads=["l0_otm", "c_ident"],
                    writes=["ps_PT"] if c in (0, 7) else [], kind="mm", dbg=f"ot_{Q}_{c}")
            P.emit("dve", (lambda qs=qs: (lambda e: e.tensor_copy(
                bass.AP(ofm, qs * 128, [[NCH * TT, 128], [TT, 8], [1, 128]]),
                bass.AP(PS["PT"], 0, [[1024, 128], [128, 8], [1, 128]]))))(),
                reads=["ps_PT"], writes=[f"l0_ofm{qs}"], dbg=f"of_{Q}")

        for oc in range(8):
            psn = f"P{oc % 4}"
            ps_ap = PS[psn][:, 0:TT]
            for c in range(8):
                P.emit("pe", (lambda oc=oc, c=c, ps_ap=ps_ap: (lambda e: e.matmul(
                    ps_ap, wout[:, c * D + oc * 128: c * D + oc * 128 + 128],
                    ofm[:, ts(c, TT)], start=(c == 0), stop=(c == 7))))(),
                    reads=[f"l0_ofm{q}" for q in range(4)] + ["l0_wout"],
                    writes=[f"ps_{psn}"] if c in (0, 7) else [], kind="mm",
                    dbg=f"o_{tt}_{oc}_{c}")
            P.emit("dve", (lambda oc=oc, ps_ap=ps_ap: (lambda e: e.tensor_scalar(
                osb[:, ts(oc, TT)], ps_ap, bout[:, oc:oc + 1], None, op0=ALU.add)))(),
                reads=[f"ps_{psn}", "l0_bout"], sreads=["l0_bout"],
                writes=[f"l0_osb{oc}"], dbg=f"ov_{tt}_{oc}")
            P.emit("act", (lambda oc=oc: (lambda e: e.activation(
                osq[:, ts(oc, TT)], osb[:, ts(oc, TT)], AF.Square, bias=0.0, scale=1.0)))(),
                reads=[f"l0_osb{oc}"], writes=[f"l0E_sq{oc}"], dbg=f"os_{tt}_{oc}")
            P.emit("pool", (lambda oc=oc: (lambda e: e.tensor_copy(
                obf[:, ts(oc, TT)], osb[:, ts(oc, TT)])))(),
                reads=[f"l0_osb{oc}"], writes=[f"l0_obf{oc}"], kind="end",
                dbg=f"oc_{tt}_{oc}")
        if DEV_DUMP and tt == 3:
            P.emit("sp", lambda e: e.dma_start(t["d_ofm"].ap(), ofm[:, :]),
                   reads=[f"l0_ofm{x}" for x in range(4)], kind="dma", dbg="dump_ofm")
            P.emit("sp", lambda e: e.dma_start(t["d_osb"].ap(), osb[:, :]),
                   reads=[f"l0_osb{x}" for x in range(8)], kind="dma", dbg="dump_osb")
        B1, B2 = _ln_stats(cx, f"l0E{tt}", "l0_obf",
                           [obf[:, ts(c, TT)] for c in range(8)],
                           "l0E_sq", [osq[:, ts(c, TT)] for c in range(8)], TT)
        xsl = fm3(x_in, tt * TT, TT)
        P.emit("sp", (lambda xsl=xsl: (lambda e: e.dma_start(
            cm3(xt, NCH * TT, TT), xsl)))(),
               writes=[f"l0_xt{c}" for c in range(8)], kind="dma", dbg=f"l2x{tt}")
        for c in range(8):
            P.emit("dve", (lambda c=c, B1=B1: (lambda e: e.scalar_tensor_tensor(
                xtmp[:, :], osb[:, ts(c, TT)], 1.0, B1,
                op0=ALU.mult, op1=ALU.mult)))(),
                reads=[f"l0_osb{c}", "ps_S1"], writes=["l0_xtmp"], dbg=f"ea_{tt}_{c}")
            P.emit("dve", (lambda c=c, B2=B2: (lambda e: e.tensor_tensor(
                xtmp[:, :], xtmp[:, :], B2, op=ALU.add)))(),
                reads=["l0_xtmp", "ps_S2"], writes=["l0_xtmp"], dbg=f"eb_{tt}_{c}")
            P.emit("dve", (lambda c=c: (lambda e: e.tensor_tensor(
                xt[:, ts(c, TT)], xt[:, ts(c, TT)], xtmp[:, :], op=ALU.add)))(),
                reads=["l0_xtmp"], writes=[f"l0_xt{c}"], dbg=f"ec_{tt}_{c}")
        P.emit("sp", (lambda tt=tt: (lambda e: e.dma_start(
            fm3(x_out, tt * TT, TT), cm3(xt, NCH * TT, TT))))(),
            reads=[f"l0_xt{c}" for c in range(8)], kind="dma", dbg=f"sx{tt}")



ALL_PHASES = ("l0_attn", "l0_ffn", "l1_attn", "l1_ffn")


def build_nc(hp, phases=ALL_PHASES):
    nc = bass.Bass("TRN2", target_bir_lowering=False)
    cx = Cx(nc)
    t = cx.t

    cx.dram("x_io", [D, S], F32, kind="ExternalInput")
    cx.dram("y_io", [D, S], F32, kind="ExternalOutput")

    ext = {
        "l0_wqkvT": [D, 3072], "l0_woutT": [D, D], "l0_bout_pt": [128, 8],
        "l0_bconv_pt": [128, 8], "l0_nullk_pt": [128, 8], "l0_nullv_pt": [128, 8],
        "l0_convw": [128, 3 * 8 * 128], "l0_band": [128, 32 * 64],
        "ident_bf": [128, 128], "ones_col": [128, 8],
    }
    ext["ones_row"] = [1, 128]
    ext["l0_rail1"] = [128, 4 * 8 * 128]
    ext["l0_rail2"] = [128, 4 * 32 * 128]
    ext["l0_wcpT"] = [16384, 1024]
    for nm in ("l1_w1T", "l1_w2T", "l1_w3T", "l1_woT"):
        ext[nm] = [D, D]
    ext["l1_cos"] = [128, S]
    ext["l1_sin"] = [128, S]
    ext["ropeP"] = [128, 128]
    for i in range(6):
        ext[f"lf_wqkv{i}"] = [D, 3072]
        ext[f"lf_wo{i}"] = [D, D]
        ext[f"lf_w1p{i}"] = [D, 5632]
        ext[f"lf_w2p{i}"] = [2816, D]
    ext["lf_b1pt"] = [128, 6 * 44]
    ext["lf_b2pt"] = [128, 48]
    ext["lf_gpt"] = [128, 48]
    ext["lf_bpt"] = [128, 48]
    ext["lf_qspt"] = [128, 6]
    ext["lf_kspt"] = [128, 6]
    ext["xp_cosS"] = [128, S]
    ext["xp_sinS"] = [128, S]
    ext["xp_cosI"] = [128, S]
    ext["xp_sinI"] = [128, S]
    ext["lf_ropeH"] = [128, 128]
    ext["lf_ones2"] = [128, 2]
    ext["lf_ones2T"] = [2, 128]
    ext["lf_mask"] = [128, 3 * 384]
    ext["cvals"] = [128, 8]
    for name, shape in ext.items():
        cx.dram(name, shape, F32, kind="ExternalInput")

    sk = "ExternalOutput" if DEV_DUMP else None
    cx.dram("sc_xn", [D, S], BF16, kind=sk)
    cx.dram("sc_wcp", [16384, 1024], BF16)
    cx.dram("sc_ya", [D, S], F32, kind=sk)
    cx.dram("sc_guard", [D, S], F32, kind=sk)
    cx.dram("sc_yb", [D, S], F32, kind=sk)
    cx.dram("sc_w2f", [8 * 128, 22 * 128], BF16)
    cx.dram("sc_otm", [S, D], BF16, kind=("ExternalOutput" if DEV_DUMP else None))
    if DEV_DUMP:
        cx.dram("d_kp", [128, NCH * S], BF16, kind="ExternalOutput")
        cx.dram("d_xn", [128, NCH * TT], BF16, kind="ExternalOutput")
        cx.dram("d_xs", [128, NCH * TT], BF16, kind="ExternalOutput")
        cx.dram("d_hb", [128, 22 * TT], BF16, kind="ExternalOutput")
        cx.dram("d_w2f", [8 * 128, 22 * 128], BF16, kind="ExternalOutput")
        cx.dram("d_vtmB", [128, 32 * D], BF16, kind="ExternalOutput")
        cx.dram("d_qbf1", [128, NCH * TT], BF16, kind="ExternalOutput")
    cx.dram("sc_kraw", [D, S], BF16, kind=sk)
    cx.dram("sc_vraw", [D, S], BF16, kind=sk)
    if DEV_DUMP:
        cx.dram("d_kc", [128, 8 * 1367], BF16, kind="ExternalOutput")
        cx.dram("d_convw", [128, 3072], BF16, kind="ExternalOutput")
        cx.dram("d_pkq", [128, 11 * 128], BF16, kind="ExternalOutput")
        cx.dram("d_otm", [128, 1024], BF16, kind="ExternalOutput")
        cx.dram("d_ktile", [128, 8 * 1540], BF16, kind="ExternalOutput")
        cx.dram("d_vtm", [128, 11 * 1024], BF16, kind="ExternalOutput")
        cx.dram("d_psb", [128, 8 * 1367], BF16, kind="ExternalOutput")
        cx.dram("d_den", [128, 16], F32, kind="ExternalOutput")
        cx.dram("d_qbf", [128, 8 * 512], BF16, kind="ExternalOutput")
        cx.dram("d_ofm", [128, 8 * 512], BF16, kind="ExternalOutput")
        cx.dram("d_osb", [128, 8 * 512], F32, kind="ExternalOutput")
    for i in range(3):
        cx.dram(f"sc_x{i}", [D, S], F32)

    es = ExitStack()
    with es:
        PS = {}
        for n in ("P0", "P1", "P2", "P3", "P4", "S1", "S2"):
            PS[n] = es.enter_context(nc.psum_tensor(n, [128, 512], F32))
        PS["PT"] = es.enter_context(nc.psum_tensor("PT", [128, 1024], BF16))
        t["psum"] = PS

        cx.init_arena()
        ident = cx.sb("c_ident", [128, 128], BF16)
        onesb = cx.sb("c_ones_bf", [128, 8], BF16)
        for n in ("st_mu", "st_ms", "st_var", "st_r", "st_nmr", "st_tmp"):
            cx.sb(n, [1, TT], F32)
        cx.sb("st_mx", [128, 8], F32)
        cx.sb("st_acc", [128, 8], F32)

        P = cx.P
        P.emit("pool", lambda e: e.dma_start(ident[:, :], t["ident_bf"].ap()),
               writes=["c_ident"], kind="dma", dbg="ld_ident")
        P.emit("pool", lambda e: e.dma_start(onesb[:, :], t["ones_col"].ap()),
               writes=["c_ones"], kind="dma", dbg="ld_ones")
        onesr = cx.sb("c_ones_row", [1, 128], F32)
        P.emit("sp", lambda e: e.dma_start(onesr[:, :], t["ones_row"].ap()),
               writes=["c_ones_row"], kind="dma", dbg="ld_onesr")
        cvals = cx.sb("c_vals", [128, 8], F32)
        P.emit("sp", lambda e: e.dma_start(cvals[:, :], t["cvals"].ap()),
               writes=["c_vals"], kind="dma", dbg="ld_cvals")
        identf = cx.sb("c_identf", [128, 128], F32)
        P.emit("sp", lambda e: e.dma_start(identf[:, :], t["ident_bf"].ap()),
               writes=["c_identf"], kind="dma", dbg="ld_identf")
        ropebf = cx.sb("c_ropeP", [128, 128], BF16)
        P.emit("pool", lambda e: e.dma_start(ropebf[:, :], t["ropeP"].ap()),
               writes=["c_ropeP"], kind="dma", dbg="ld_ropeP")
        P.barrier()

        cur = t["x_io"]
        sc_i = 0
        for ph in phases:
            dst = t["y_io"] if ph == phases[-1] else t[f"sc_x{sc_i}"]
            if ph == "l0_attn":
                phase_l0_attn(cx, cur, dst, hp)
            elif ph == "l0_ffn":
                phase_l0_ffn(cx, cur, dst, hp)
            elif ph == "l1_attn":
                phase_l1_attn(cx, cur, dst, hp)
            elif ph == "l1_ffn":
                phase_l1_ffn(cx, cur, dst, hp)
            else:
                raise NotImplementedError(ph)
            cur = dst
            sc_i += 1
            P.barrier()

        runners = P.finalize(nc, es)
        block = es.enter_context(nc.Block())
        block.tensor(runners["pe"])
        block.scalar(runners["act"])
        block.vector(runners["dve"])
        block.gpsimd(runners["pool"])
        block.sync(runners["sp"])

    return nc


def _percore_inputs(inputs, hp, core):
    x = np.asarray(inputs["L0_attn_x"])[core]
    m = {"x_io": np.ascontiguousarray(np.asarray(x, dtype=np.float32).T)}
    for k, v in hp.items():
        if not k.startswith("_"):
            m[k] = v
    return m


def kernel(**inputs):
    hp = host_prep(inputs)
    nc = build_nc(hp)
    in_maps = [_percore_inputs(inputs, hp, c) for c in range(NCORES)]
    res = run_bass_kernel_spmd(nc, in_maps, core_ids=list(range(NCORES)))
    out = np.stack([np.asarray(r["y_io"]).T for r in res.results], axis=0)
    return np.ascontiguousarray(out.astype(np.float32))
